# version 19
# baseline (speedup 1.0000x reference)
import numpy as np
import ml_dtypes
from contextlib import ExitStack

import concourse.bass as bass
import concourse.mybir as mybir
from concourse.bass_utils import run_bass_kernel_spmd
from concourse.aot_env import allow_ici_all_to_all

F32 = mybir.dt.float32
BF16 = mybir.dt.bfloat16
AF = mybir.ActivationFunctionType
ALU = mybir.AluOpType
AX = mybir.AxisListType

NC_ = 8
D = 2048
NDC = D // 128
HD = 128
HPC = 2
EPS_LN = 1e-5
EPS_Q = 1e-8
SM_SCALE = float(1.0 / np.sqrt(HD))
DEBUG_TAPS = False


class Plan:
    ENGINES = ("tensor", "scalar", "vector", "gpsimd", "sync")

    def __init__(self):
        self.plan = {e: [] for e in self.ENGINES}
        self.count = {}
        self.last_wait = {}
        self.semnames = []

    def sem(self, name):
        if name not in self.count:
            self.count[name] = 0
            self.semnames.append(name)
        return name

    def inc(self, name, amt):
        self.count[name] = self.count.get(name, 0) + amt
        return self.count[name]

    def cur(self, name):
        return self.count.get(name, 0)

    def op(self, eng, fn, waits=(), inc=None, inc_amt=1):
        w = []
        for s, v in waits:
            if v <= 0:
                continue
            k = (eng, s)
            if self.last_wait.get(k, -1) >= v:
                continue
            self.last_wait[k] = v
            w.append((s, v))
        if inc is not None:
            if inc_amt is None:
                self.inc(inc, 1)
            else:
                self.inc(inc, inc_amt)
        self.plan[eng].append((w, fn, inc, inc_amt))

    def dma(self, eng, stream, fn, waits=()):
        self.op(eng, fn, waits=waits, inc=stream, inc_amt=16)
        return (stream, self.cur(stream))

    def run(self, eng, engobj, sems):
        for w, fn, inc, inc_amt in self.plan[eng]:
            for s, v in w:
                engobj.wait_ge(sems[s], v)
            inst = fn(engobj)
            if inc is not None:
                if inc_amt is None:
                    inst.then_inc(sems[inc])
                else:
                    inst.then_inc(sems[inc], inc_amt)


class Rot:

    def __init__(self, depth):
        self.depth = depth
        self.users = []

    def idx(self):
        return len(self.users) % self.depth

    def acquire(self):
        i = len(self.users)
        if i < self.depth:
            return []
        return [self.users[i - self.depth]]

    def commit(self, sem, val):
        self.users.append((sem, val))


def build_program(S, HFF):
    TMY = S // NC_
    NTT = TMY // 128
    TCH = min(512, TMY)
    NTC = TMY // TCH
    NQC = S // 512
    NJC = HFF // 128
    W12SH = D // NC_
    W3SH = HFF // NC_
    NDO = D // 512
    CH = HPC * HD * TMY

    nc = bass.Bass("TRN2", target_bir_lowering=False, debug=False)

    x_my = nc.dram_tensor("x_my", [TMY, D], F32, kind="ExternalInput").ap()
    wqkv_my = nc.dram_tensor("wqkv_my", [D, 6 * HD], F32, kind="ExternalInput").ap()
    wout = nc.dram_tensor("wout", [D, D], F32, kind="ExternalInput").ap()
    bout = nc.dram_tensor("bout", [1, D], F32, kind="ExternalInput").ap()
    w1_my = nc.dram_tensor("w1_my", [HFF, W12SH], F32, kind="ExternalInput").ap()
    w2_my = nc.dram_tensor("w2_my", [HFF, W12SH], F32, kind="ExternalInput").ap()
    w3_my = nc.dram_tensor("w3_my", [D, W3SH], F32, kind="ExternalInput").ap()
    ident_in = nc.dram_tensor("ident", [128, 128], BF16, kind="ExternalInput").ap()
    mask_in = nc.dram_tensor("maskd", [512, 512], BF16, kind="ExternalInput").ap()
    oneskc_in = nc.dram_tensor("oneskc", [128, 1], BF16, kind="ExternalInput").ap()
    onesr_in = nc.dram_tensor("onesr", [1, 128], BF16, kind="ExternalInput").ap()
    onesf_in = nc.dram_tensor("onesf", [1, 128], F32, kind="ExternalInput").ap()

    y_my = nc.dram_tensor("y_my", [TMY, D], F32, kind="ExternalOutput").ap()

    TH = TMY // NTC
    xn_sh = [nc.dram_tensor(f"xn_sh{i}", [D, TH], BF16) for i in range(NTC)]
    xn_ag = [nc.dram_tensor(f"xn_ag{i}", [NC_ * D, TH], BF16,
                            addr_space="Shared") for i in range(NTC)]
    gam_in = nc.dram_tensor("gam_in", [1, 4], F32)
    gam_out = nc.dram_tensor("gam_out", [1, 4], F32, addr_space="Shared")
    w1t_sh = nc.dram_tensor("w1t_sh", [W12SH, HFF], BF16)
    w1t_ag = nc.dram_tensor("w1t_ag", [D, HFF], BF16, addr_space="Shared")
    w2t_sh = nc.dram_tensor("w2t_sh", [W12SH, HFF], BF16)
    w2t_ag = nc.dram_tensor("w2t_ag", [D, HFF], BF16, addr_space="Shared")
    w3t_sh = nc.dram_tensor("w3t_sh", [W3SH, D], BF16)
    w3t_ag = nc.dram_tensor("w3t_ag", [HFF, D], BF16, addr_space="Shared")
    ctx_sh = nc.dram_tensor("ctx_sh", [NC_, CH], BF16)
    ctx_a2a = nc.dram_tensor("ctx_a2a", [NC_, CH], BF16)
    x1res_d = nc.dram_tensor("x1res_d", [TMY, D], F32)
    hidden_d = nc.dram_tensor("hidden_d", [HFF, TMY], BF16)
    if DEBUG_TAPS:
        dbg_ctx = nc.dram_tensor("dbg_ctx", [NC_, CH], BF16, kind="ExternalOutput")
        dbg_ctxsh = nc.dram_tensor("dbg_ctxsh", [NC_, CH], BF16, kind="ExternalOutput")
        dbg_x1res = nc.dram_tensor("dbg_x1res", [TMY, D], F32, kind="ExternalOutput")
        dbg_hid = nc.dram_tensor("dbg_hid", [HFF, TMY], BF16, kind="ExternalOutput")
        dbg_w1t = nc.dram_tensor("dbg_w1t", [D, HFF], BF16, kind="ExternalOutput")
        dbg_w3t = nc.dram_tensor("dbg_w3t", [HFF, D], BF16, kind="ExternalOutput")
        dbg_gam = nc.dram_tensor("dbg_gam", [1, 4], F32, kind="ExternalOutput")
        dbg_qkvsb = nc.dram_tensor("dbg_qkvsb", [128, 6 * S], BF16, kind="ExternalOutput")

    P = Plan()
    for s in ("sc", "pv", "exp", "msk", "act", "dve", "pe", "pool"):
        P.sem(s)
    DSEMS = ["con", "gcon", "xld", "xnst", "gaml", "gamd", "qld",
             "xb0s", "xb0a", "xb1s", "xb1a",
             "cst0", "cst1", "ctxld", "rp0", "rp1", "x1st0", "x1st1",
             "wl0", "wl1", "hst0", "hst1", "hb0", "hb1", "hb2", "hb3",
             "w30", "w31", "w32", "w33",
             "yst0", "yst1", "qst", "wo0", "wo1", "dbg", "ctxld2",
             "rp2", "rp3", "rp4", "rp5", "rp6", "rp7", "xnt0", "xnt1",
             "gamlb", "xld2"]
    CSEMS = ["ccxn", "ccgam", "ccw1", "ccw2", "ccw3", "cca2a"]
    for s in DSEMS + CSEMS:
        P.sem(s)

    with ExitStack() as ctx:
        E = ctx.enter_context
        arena = E(nc.sbuf_tensor("arena", [128, 6 * S], BF16))
        Q0, K0, V0 = 0, 2 * S, 4 * S
        XT0 = 2 * S
        stream = E(nc.sbuf_tensor("stream", [128, 16384], BF16))
        misc = E(nc.sbuf_tensor("misc", [128, 8192], BF16))
        scratch = E(nc.sbuf_tensor("scratch", [128, 4096], F32))
        wqkv_sb = E(nc.sbuf_tensor("wqkv_sb", [128, NDC * 6 * HD], BF16))
        maskd = E(nc.sbuf_tensor("maskd_sb", [128, 2048], BF16))
        ident = E(nc.sbuf_tensor("ident_sb", [128, 128], BF16))
        oneskc = E(nc.sbuf_tensor("oneskc_sb", [128, 1], BF16))
        onesr = E(nc.sbuf_tensor("onesr_sb", [1, 128], BF16))
        onesf = E(nc.sbuf_tensor("onesf_sb", [1, 128], F32))
        bout_bf = E(nc.sbuf_tensor("bout_bf", [1, D], BF16))
        lnf = E(nc.sbuf_tensor("lnf", [128, 32], F32))
        den_sb = [E(nc.sbuf_tensor(f"den_sb{h}", [1, 512], F32)) for h in range(2)]
        rden = [E(nc.sbuf_tensor(f"rden{h}", [1, 512], F32)) for h in range(2)]
        gamr = E(nc.sbuf_tensor("gamr", [1, 4], F32))
        scpA = E(nc.psum_tensor("scpA", [128, 1024], F32))
        scpB = E(nc.psum_tensor("scpB", [128, 1024], F32))
        ctxp = [E(nc.psum_tensor(f"ctxp{h}", [128, 512], F32)) for h in range(2)]
        denp = [E(nc.psum_tensor(f"denp{h}", [128, 512], F32)) for h in range(2)]
        sems = {s: E(nc.semaphore(s)) for s in P.semnames}

        scp_t = [scpA, scpB]
        pb = [scpA[:, 0:512], scpA[:, 512:1024], scpB[:, 0:512], scpB[:, 512:1024],
              ctxp[0][:, :], ctxp[1][:, :], denp[0][:, :], denp[1][:, :]]
        pbbf = [scpA[:, 0:512].bitcast(BF16), scpB[:, 0:512].bitcast(BF16)]

        P.dma("sync", "con", lambda s: s.dma_start(out=ident[:, :], in_=ident_in))
        P.dma("sync", "con", lambda s: s.dma_start(
            out=maskd[:, :].rearrange("p (kt q) -> p kt q", kt=4),
            in_=mask_in.rearrange("(kt p) q -> p kt q", p=128)))
        P.dma("sync", "con", lambda s: s.dma_start(out=oneskc[:, :], in_=oneskc_in))
        P.dma("sync", "con", lambda s: s.dma_start(out=onesr[:, :], in_=onesr_in))
        P.dma("sync", "con", lambda s: s.dma_start(out=onesf[:, :], in_=onesf_in))
        P.dma("gpsimd", "gcon", lambda g: g.dma_start(out=bout_bf[:, :], in_=bout))
        P.dma("gpsimd", "gcon", lambda g: g.dma_start(
            out=wqkv_sb[:, :].rearrange("p (dc o) -> p dc o", dc=NDC),
            in_=wqkv_my.rearrange("(dc p) o -> p dc o", p=128)))
        CONS = ("con", P.cur("con"))
        GCONS = ("gcon", P.cur("gcon"))

        def emit_ln_tile(out_bf_tile, ld_waits, apply_waits=(), slot=0):
            src = scratch[:, slot * 2048:(slot + 1) * 2048]
            cb = slot * 16
            c = lambda i: lnf[:, cb + i:cb + i + 1]
            P.op("vector", lambda v: v.reduce_sum(c(0), src, axis=AX.X),
                 waits=list(ld_waits) + [("dve", P.cur("dve"))], inc="dve")
            P.op("scalar", lambda sc, o=out_bf_tile: sc.activation(
                out=o, in_=src, func=AF.Square,
                accum_out=c(1)), waits=list(ld_waits) + list(apply_waits),
                inc="act")
            act_sq = P.cur("act")
            P.op("vector", lambda v: v.tensor_scalar(
                out=c(2), in0=c(0), scalar1=1.0 / D, scalar2=None,
                op0=ALU.mult), waits=[("dve", P.cur("dve"))], inc="dve")
            P.op("vector", lambda v: v.tensor_tensor(
                out=c(3), in0=c(2), in1=c(2), op=ALU.mult),
                waits=[("dve", P.cur("dve"))], inc="dve")
            P.op("vector", lambda v: v.scalar_tensor_tensor(
                out=c(4), in0=c(1), scalar=1.0 / D, in1=c(3),
                op0=ALU.mult, op1=ALU.subtract),
                waits=[("act", act_sq), ("dve", P.cur("dve"))], inc="dve")
            P.op("vector", lambda v: v.tensor_scalar(
                out=c(4), in0=c(4), scalar1=EPS_LN, scalar2=None,
                op0=ALU.add), waits=[("dve", P.cur("dve"))], inc="dve")
            P.op("scalar", lambda sc: sc.activation(
                out=c(5), in_=c(4), func=AF.Sqrt),
                waits=[("dve", P.cur("dve"))], inc="act")
            P.op("vector", lambda v: v.reciprocal(c(6), c(5)),
                 waits=[("act", P.cur("act")), ("dve", P.cur("dve"))], inc="dve")
            P.op("vector", lambda v: v.scalar_tensor_tensor(
                out=c(7), in0=c(2), scalar=-1.0, in1=c(6),
                op0=ALU.mult, op1=ALU.mult),
                waits=[("dve", P.cur("dve"))], inc="dve")
            P.op("scalar", lambda sc, o=out_bf_tile: sc.activation(
                out=o, in_=src, func=AF.Identity, scale=c(6), bias=c(7)),
                waits=[("dve", P.cur("dve"))], inc="act")
            return P.cur("act"), P.cur("dve")

        w1flat = w1_my.rearrange("(g p) d -> p g d", p=128)
        w2flat = w2_my.rearrange("(g p) d -> p g d", p=128)
        w3flat = w3_my.rearrange("(dc p) j -> p dc j", p=128)
        G12 = HFF // 128

        def emit_gamma(widx, src_ap, nblocks, bw, first):
            b0 = 0
            while b0 < nblocks:
                nb = min(max(1, 2048 // bw), nblocks - b0)
                cw = nb * bw
                ld = P.dma("scalar", "gaml", lambda s, a=src_ap, b0=b0, nb=nb, bw=bw, cw=cw: s.dma_start(
                    out=scratch[:, 0:cw].rearrange("p (a b) -> p a b", b=bw),
                    in_=a[:, b0:b0 + nb, :]),
                    waits=[("dve", P.cur("dve")), ("act", P.cur("act"))])
                P.op("vector", lambda v, cw=cw: v.reduce_sum(
                    lnf[:, 11:12], scratch[:, 0:cw], axis=AX.X,
                    apply_absolute_value=True),
                    waits=[ld, ("dve", P.cur("dve"))], inc="dve")
                if first and b0 == 0:
                    P.op("vector", lambda v, w=widx: v.tensor_copy(
                        lnf[:, 8 + w:9 + w], lnf[:, 11:12]),
                        waits=[("dve", P.cur("dve"))], inc="dve")
                else:
                    P.op("vector", lambda v, w=widx: v.tensor_tensor(
                        out=lnf[:, 8 + w:9 + w], in0=lnf[:, 8 + w:9 + w],
                        in1=lnf[:, 11:12], op=ALU.add),
                        waits=[("dve", P.cur("dve"))], inc="dve")
                b0 += nb

        gamma_units = []

        def gamma_unit_blocks(widx, src_ap, nblocks, bw, first):
            b0 = 0
            while b0 < nblocks:
                nb = min(max(1, 2048 // bw), nblocks - b0)
                gamma_units.append(
                    lambda widx=widx, a=src_ap, b0=b0, nb=nb, bw=bw,
                    fi=(first and b0 == 0): emit_gamma_blk(widx, a, b0, nb, bw, fi))
                b0 += nb

        gstg = stream[:, 8192:12288].bitcast(F32)

        gam_q = [0]

        def emit_gamma_blk(widx, src_ap, b0, nb, bw, first):
            cw = nb * bw
            gq = ("scalar", "gaml") if gam_q[0] % 2 == 0 else ("sync", "gamlb")
            gam_q[0] += 1
            ld = P.dma(gq[0], gq[1], lambda s, a=src_ap, b0=b0, nb=nb, bw=bw, cw=cw: s.dma_start(
                out=gstg[:, 0:cw].rearrange("p (a b) -> p a b", b=bw),
                in_=a[:, b0:b0 + nb, :]),
                waits=[("dve", P.cur("dve")), ("act", P.cur("act"))])
            P.op("vector", lambda v, cw=cw: v.reduce_sum(
                lnf[:, 11:12], gstg[:, 0:cw], axis=AX.X,
                apply_absolute_value=True),
                waits=[ld, ("dve", P.cur("dve"))], inc="dve")
            if first:
                P.op("vector", lambda v, w=widx: v.tensor_copy(
                    lnf[:, 8 + w:9 + w], lnf[:, 11:12]),
                    waits=[("dve", P.cur("dve"))], inc="dve")
            else:
                P.op("vector", lambda v, w=widx: v.tensor_tensor(
                    out=lnf[:, 8 + w:9 + w], in0=lnf[:, 8 + w:9 + w],
                    in1=lnf[:, 11:12], op=ALU.add),
                    waits=[("dve", P.cur("dve"))], inc="dve")

        gamma_unit_blocks(0, w1flat, G12, W12SH, True)
        gamma_unit_blocks(1, w2flat, G12, W12SH, True)
        gamma_unit_blocks(2, w3flat, NDC, W3SH, True)

        def emit_gamma_tail():
            P.op("vector", lambda v: v.memset(gamr[0:1, :], 0.0),
                 waits=[("dve", P.cur("dve"))], inc="dve")
            P.op("gpsimd", lambda g: g.tensor_reduce(
                out=gamr[0:1, 0:3], in_=lnf[:, 8:11], axis=AX.C, op=ALU.add),
                waits=[("dve", P.cur("dve"))], inc="pool")
            g1 = P.dma("gpsimd", "gamd", lambda g: g.dma_start(
                out=gam_in.ap(), in_=gamr[0:1, :]),
                waits=[("pool", P.cur("pool"))])
            P.op("gpsimd", lambda g: g.collective_compute(
                "AllReduce", ALU.add, replica_groups=[list(range(NC_))],
                ins=[gam_in.ap().opt()], outs=[gam_out.ap().opt()]),
                waits=[g1], inc="ccgam", inc_amt=None)
            g2 = P.dma("gpsimd", "gamd", lambda g: g.dma_start(
                out=gamr[0:1, :], in_=gam_out.ap()),
                waits=[("ccgam", 1)])
            P.op("vector", lambda v: v.tensor_scalar(
                out=gamr[0:1, :], in0=gamr[0:1, :], scalar1=1.0 / (HFF * D * 2),
                scalar2=EPS_Q / 2, op0=ALU.mult, op1=ALU.add),
                waits=[g2, ("dve", P.cur("dve"))], inc="dve")
            thr_dve = P.cur("dve")
            P.op("tensor", lambda t: t.matmul(
                denp[0][:, 0:4], onesf[:, :], gamr[0:1, :], start=True, stop=True),
                waits=[("dve", thr_dve), CONS], inc="pe")
            P.op("vector", lambda v: v.tensor_copy(lnf[:, 12:16], denp[0][:, 0:4]),
                 waits=[("pe", P.cur("pe")), ("dve", P.cur("dve"))], inc="dve")
            P.op("vector", lambda v: v.tensor_scalar(
                out=lnf[:, 8:11], in0=lnf[:, 12:15], scalar1=-1.0, scalar2=None,
                op0=ALU.mult), waits=[("dve", P.cur("dve"))], inc="dve")
            return ("dve", P.cur("dve"))
        thr_consumed = [("dve", 0)]

        TPP = TH // 128
        ccxn_n = [0]
        bf_rot = Rot(2)
        ln_tr_rot = bf_rot
        xstg_free = [("pe", 0), ("pe", 0)]
        xn_stage_free = ("pe", 0)
        pending_lntr = [None]
        xnt_sems = ["xnt0", "xnt1"]

        def flush_lntr():
            fn = pending_lntr[0]
            pending_lntr[0] = None
            if fn is not None:
                fn()

        def emit_lntr(tt, slot, a_done):
            stg = 4096 + slot * 2048
            for dc in range(NDC):
                tview = pbbf[ln_tr_rot.idx()][:, 0:128]
                aw = ln_tr_rot.acquire()
                P.op("tensor", lambda t, dc=dc, tv=tview, slot=slot: t.transpose(
                    tv, misc[:, slot * 2048 + dc * 128: slot * 2048 + (dc + 1) * 128],
                    ident[:, :]),
                    waits=[("act", a_done), CONS] + aw, inc="pe")
                tr = P.cur("pe")
                P.op("vector", lambda v, dc=dc, tv=tview, stg=stg: v.tensor_copy(
                    misc[:, stg + dc * 128: stg + (dc + 1) * 128], tv),
                    waits=[("pe", tr), ("dve", P.cur("dve")), xstg_free[slot]],
                    inc="dve")
                ln_tr_rot.commit("dve", P.cur("dve"))
            cp_done = P.cur("dve")
            part, off = tt // TPP, tt % TPP
            st = P.dma("sync", xnt_sems[slot], lambda s, p=part, o=off, stg=stg: s.dma_start(
                out=xn_sh[p].ap()[:, o * 128:(o + 1) * 128].rearrange(
                    "(dc p2) t -> p2 dc t", p2=128),
                in_=misc[:, stg:stg + 2048].rearrange("p (dc t) -> p dc t", dc=NDC)),
                waits=[("dve", cp_done)])
            xstg_free[slot] = st
            if off == TPP - 1:
                P.op("gpsimd", lambda g, p=part: g.collective_compute(
                    "AllGather", ALU.bypass, replica_groups=[list(range(NC_))],
                    ins=[xn_sh[p].ap().opt()], outs=[xn_ag[p].ap().opt()]),
                    waits=[("xnt0", P.cur("xnt0")), ("xnt1", P.cur("xnt1"))],
                    inc="ccxn", inc_amt=None)
                ccxn_n[0] += 1

        xslot_free = [("pe", 0), ("pe", 0)]
        for tt in range(NTT):
            slot = tt % 2
            xq = ("sync", "xld") if slot == 0 else ("scalar", "xld2")
            ld = P.dma(xq[0], xq[1], lambda s, tt=tt, slot=slot: s.dma_start(
                out=scratch[:, slot * 2048:(slot + 1) * 2048],
                in_=x_my[tt * 128:(tt + 1) * 128, :]),
                waits=[xslot_free[slot]])
            a_done, d_done = emit_ln_tile(
                misc[:, slot * 2048:(slot + 1) * 2048], [ld],
                apply_waits=[xn_stage_free], slot=slot)
            xslot_free[slot] = ("act", a_done)
            prev = pending_lntr[0]
            pending_lntr[0] = (lambda tt=tt, slot=slot, a=a_done: emit_lntr(tt, slot, a))
            if prev is not None:
                prev()
                xn_stage_free = ("pe", P.cur("pe"))
            ng_ = len(gamma_units)
            for u in gamma_units[(tt * ng_) // NTT:((tt + 1) * ng_) // NTT]:
                u()
        flush_lntr()
        CCXN = ("ccxn", ccxn_n[0])


        qkv_rot = Rot(2)
        vtr_rot = bf_rot
        sbuf_rot = Rot(2)
        SW = NDC * TCH
        vtmp_free = [("pe", 0), ("pe", 0)]
        pending_vtr = [None]

        def qkv_chunk(r, tc, extra=()):
            base_tok = r * TMY + tc * TCH
            sslot = sbuf_rot.idx()
            sview = stream[:, sslot * SW:(sslot + 1) * SW]
            aw = sbuf_rot.acquire()
            xbsem = "xb0s" if sslot == 0 else "xb1s"
            P.dma("sync", xbsem, lambda e, r=r, tc=tc, sv=sview: e.dma_start(
                out=sv.rearrange("p (dc t) -> p dc t", dc=NDC),
                in_=xn_ag[tc].ap()[r * D:(r + 1) * D, :].rearrange(
                    "(dc p) t -> p dc t", p=128)),
                waits=[("ccxn", tc + 1)] + aw + list(extra))
            ld2 = [(xbsem, P.cur(xbsem))]
            last_pe = 0
            for o in range(6):
                slot = qkv_rot.idx()
                ps = pb[4 + slot]
                aw2 = qkv_rot.acquire()
                for dc in range(NDC):
                    P.op("tensor", lambda t, dc=dc, o=o, ps=ps, sv=sview: t.matmul(
                        ps[:, 0:TCH],
                        wqkv_sb[:, dc * 768 + o * 128: dc * 768 + (o + 1) * 128],
                        sv[:, dc * TCH:(dc + 1) * TCH],
                        start=(dc == 0), stop=(dc == NDC - 1)),
                        waits=(ld2 + [GCONS, CONS] + aw2) if dc == 0 else [],
                        inc="pe" if dc == NDC - 1 else None)
                mm_done = P.cur("pe")
                last_pe = mm_done
                if o < 4:
                    h = o % 2
                    dst0 = (Q0 if o < 2 else K0) + h * S + base_tok
                    P.op("scalar", lambda sc, ps=ps, d0=dst0: sc.activation(
                        out=arena[:, d0:d0 + TCH], in_=ps[:, 0:TCH], func=AF.Copy),
                        waits=[("pe", mm_done)], inc="act")
                    qkv_rot.commit("act", P.cur("act"))
                else:
                    h = o - 4
                    vtmp = misc[:, h * TCH:(h + 1) * TCH]
                    if pending_vtr[0] is not None:
                        pending_vtr[0]()
                        pending_vtr[0] = None
                    P.op("vector", lambda v, ps=ps, vt=vtmp: v.tensor_copy(vt, ps[:, 0:TCH]),
                         waits=[("pe", mm_done), ("dve", P.cur("dve")),
                                vtmp_free[h]],
                         inc="dve")
                    ev_done = P.cur("dve")
                    qkv_rot.commit("dve", ev_done)

                    def emit_vtr(h=h, vtmp=vtmp, ev_done=ev_done, base_tok=base_tok):
                        for sub in range(TCH // 128):
                            ktg = (base_tok + sub * 128) // 128
                            tview = pbbf[vtr_rot.idx()][:, 0:128]
                            P.op("tensor", lambda t, sub=sub, tv=tview, vt=vtmp: t.transpose(
                                tv, vt[:, sub * 128:(sub + 1) * 128], ident[:, :]),
                                waits=[("dve", ev_done)] + vtr_rot.acquire(), inc="pe")
                            tr_done = P.cur("pe")
                            P.op("vector", lambda v, tv=tview, ktg=ktg, h=h: v.tensor_copy(
                                arena[:, V0 + ktg * 256 + h * 128: V0 + ktg * 256 + (h + 1) * 128],
                                tv),
                                waits=[("pe", tr_done), ("dve", P.cur("dve"))], inc="dve")
                            vtr_rot.commit("dve", P.cur("dve"))
                        vtmp_free[h] = ("pe", P.cur("pe"))
                    pending_vtr[0] = emit_vtr
                    last_pe = P.cur("pe")
            sbuf_rot.commit("pe", last_pe)

        qz_read_free = [("xnt0", P.cur("xnt0")), ("xnt1", P.cur("xnt1"))]
        stage_free = [[("xnt0", P.cur("xnt0")), ("xnt1", P.cur("xnt1"))]]

        def emit_quant_tile(widx, nco, ld):
            f32_view = scratch[:, 0:nco]
            q1 = scratch[:, 2048:2048 + nco]
            q2 = misc[:, 4096:4096 + nco]
            P.op("vector", lambda v, a=f32_view, q1=q1, w=widx: v.tensor_scalar(
                out=q1, in0=a, scalar1=lnf[:, 12 + w:13 + w], scalar2=None,
                op0=ALU.is_ge),
                waits=[ld, ("dve", max(thr_consumed[0][1], P.cur("dve")))],
                inc="dve")
            P.op("vector", lambda v, a=f32_view, q2=q2, w=widx: v.tensor_scalar(
                out=q2, in0=a, scalar1=lnf[:, 8 + w:9 + w], scalar2=None,
                op0=ALU.is_le),
                waits=[("dve", P.cur("dve"))] + list(qz_read_free), inc="dve")
            P.op("vector", lambda v, q1=q1, q2=q2: v.tensor_tensor(
                out=q2, in0=q1, in1=q2, op=ALU.subtract),
                waits=[("dve", P.cur("dve"))], inc="dve")
            return P.cur("dve")

        def quant_unit_w12(widx, flat_ap, dst_sh, g0, nb):
            nco = nb * W12SH
            ld = P.dma("scalar", "qld", lambda s, a=flat_ap, g0=g0, nb=nb: s.dma_start(
                out=scratch[:, 0:nb * W12SH].rearrange("p (a b) -> p a b", b=W12SH),
                in_=a[:, g0:g0 + nb, :]),
                waits=[("act", P.cur("act")), ("dve", P.cur("dve"))])
            qz_pool = emit_quant_tile(widx, nco, ld)
            return lambda: quant_tr_w12(dst_sh, g0, nb, qz_pool)

        def quant_tr_w12(dst_sh, g0, nb, qz_pool):
            copies = []
            for dh in range(2):
                for gi in range(nb):
                    tview = pbbf[vtr_rot.idx()][:, 0:128]
                    src = misc[:, 4096 + gi * W12SH + dh * 128:
                               4096 + gi * W12SH + (dh + 1) * 128]
                    P.op("tensor", lambda t, tv=tview, s=src: t.transpose(
                        tv, s, ident[:, :]),
                        waits=[("dve", qz_pool)] + vtr_rot.acquire(), inc="pe")
                    tr = P.cur("pe")
                    P.op("vector", lambda v, tv=tview, dh=dh, gi=gi: v.tensor_copy(
                        misc[:, 6144 + dh * 1024 + gi * 128:
                             6144 + dh * 1024 + (gi + 1) * 128], tv),
                        waits=[("pe", tr), ("dve", P.cur("dve"))] +
                        (stage_free[0] if isinstance(stage_free[0], list)
                         else [stage_free[0]]),
                        inc="dve")
                    vtr_rot.commit("dve", P.cur("dve"))
                copies.append(P.cur("dve"))
            qz_read_free[:] = [("pe", P.cur("pe"))]
            for dh in range(2):
                P.dma("gpsimd", "qst", lambda g, dh=dh, g0=g0, nb=nb, d=dst_sh: g.dma_start(
                    out=d.ap()[dh * 128:(dh + 1) * 128, g0 * 128:(g0 + nb) * 128],
                    in_=misc[:, 6144 + dh * 1024: 6144 + dh * 1024 + nb * 128]),
                    waits=[("dve", copies[dh])])
            stage_free[0] = ("qst", P.cur("qst"))

        def quant_unit_w3(dcp):
            nco = 2 * W3SH
            ld = P.dma("scalar", "qld", lambda s, dcp=dcp: s.dma_start(
                out=scratch[:, 0:2 * W3SH].rearrange("p (a b) -> p a b", b=W3SH),
                in_=w3flat[:, dcp * 2:dcp * 2 + 2, :]),
                waits=[("act", P.cur("act")), ("dve", P.cur("dve"))])
            qz_pool = emit_quant_tile(2, nco, ld)
            return lambda: quant_tr_w3(dcp, qz_pool)

        def quant_tr_w3(dcp, qz_pool):
            njch = (W3SH + 127) // 128
            last_pe = 0
            for dci in range(2):
                dc = dcp * 2 + dci
                for jc in range(njch):
                    jw = min(128, W3SH - jc * 128)
                    tview = pbbf[vtr_rot.idx()][0:jw, 0:128]
                    src = misc[:, 4096 + dci * W3SH + jc * 128:
                               4096 + dci * W3SH + jc * 128 + jw]
                    P.op("tensor", lambda t, tv=tview, s=src: t.transpose(
                        tv, s, ident[:, :]),
                        waits=[("dve", qz_pool)] + vtr_rot.acquire(), inc="pe")
                    last_pe = P.cur("pe")
                    P.op("vector", lambda v, tv=tview, jw=jw: v.tensor_copy(
                        misc[0:jw, 6144:6272], tv),
                        waits=[("pe", last_pe), ("dve", P.cur("dve"))] +
                        (stage_free[0] if isinstance(stage_free[0], list)
                         else [stage_free[0]]),
                        inc="dve")
                    vtr_rot.commit("dve", P.cur("dve"))
                    P.dma("gpsimd", "qst", lambda g, dc=dc, jc=jc, jw=jw: g.dma_start(
                        out=w3t_sh.ap()[jc * 128: jc * 128 + jw, dc * 128:(dc + 1) * 128],
                        in_=misc[0:jw, 6144:6272]),
                        waits=[("dve", P.cur("dve"))])
                    stage_free[0] = [("qst", P.cur("qst"))]
            qz_read_free[:] = [("pe", last_pe)]

        quant_units = []
        for widx, (flat, dst) in enumerate(((w1flat, w1t_sh), (w2flat, w2t_sh))):
            g0 = 0
            while g0 < G12:
                nb = min(8, G12 - g0)
                quant_units.append(
                    lambda widx=widx, flat=flat, dst=dst, g0=g0, nb=nb:
                    quant_unit_w12(widx, flat, dst, g0, nb))
                g0 += nb
        for dcp in range(NDC // 2):
            quant_units.append(lambda dcp=dcp: quant_unit_w3(dcp))

        chunks = [(r, tc) for tc in range(NTC) for r in range(NC_)]
        nchunks = len(chunks)
        sched = {}
        sched.setdefault(0, []).append("TAIL")
        nq = len(quant_units)
        q0 = min(3, nchunks - 2)
        span = max(1, nchunks - q0)
        for i, u in enumerate(quant_units):
            sched.setdefault(q0 + (i * span) // max(1, nq), []).append(u)
        pending_qtr = []

        def run_units(us):
            for u in us:
                if u == "TAIL":
                    thr_consumed[0] = emit_gamma_tail()
                else:
                    if pending_qtr:
                        pending_qtr.pop(0)()
                    r = u()
                    if callable(r):
                        pending_qtr.append(r)
        gam_guard = [("dve", P.cur("dve"))]
        for i, (r, tc) in enumerate(chunks):
            qkv_chunk(r, tc, extra=gam_guard if i == 1 else ())
            run_units(sched.pop(i, []))
        for key in sorted(sched):
            run_units(sched[key])
        while pending_qtr:
            pending_qtr.pop(0)()
        if pending_vtr[0] is not None:
            pending_vtr[0]()
            pending_vtr[0] = None
        qkv_pe = P.cur("pe")
        qkv_act = P.cur("act")
        qkv_dve = P.cur("dve")
        QSTF = ("qst", P.cur("qst"))
        for (sh, ag, cs) in ((w1t_sh, w1t_ag, "ccw1"), (w2t_sh, w2t_ag, "ccw2"),
                             (w3t_sh, w3t_ag, "ccw3")):
            P.op("gpsimd", lambda g, sh=sh, ag=ag: g.collective_compute(
                "AllGather", ALU.bypass, replica_groups=[list(range(NC_))],
                ins=[sh.ap().opt()], outs=[ag.ap().opt()]),
                waits=[QSTF], inc=cs, inc_amt=None)
        CCW = [("ccw1", 1), ("ccw2", 1), ("ccw3", 1)]

        scp_rot = Rot(2)
        p_rot = Rot(4)
        PSL = 1024

        ctx_free = [("pe", 0), ("pe", 0)]
        den_free = [thr_consumed[0], ("pe", 0)]
        norm_stage_free = [("cst0", 0), ("cst1", 0)]
        den_sb_busy = [("pe", 0), ("pe", 0)]
        scratch_busy = [("pe", 0), ("pe", 0)]
        rden_busy = [("pe", 0), ("pe", 0)]
        pending_norm = []

        def flush_norm(slack):
            while len(pending_norm) > slack:
                pending_norm.pop(0)()

        def emit_normalize(qc, h, last_pv):
            P.op("vector", lambda v: v.tensor_copy(den_sb[h][:, :], denp[h][0:1, :]),
                 waits=[("pv", last_pv), ("dve", P.cur("dve")), den_sb_busy[h]],
                 inc="dve")
            den_free[h] = ("dve", P.cur("dve"))
            P.op("vector", lambda v: v.tensor_copy(
                scratch[:, h * 512:(h + 1) * 512], ctxp[h][:, :]),
                waits=[("dve", P.cur("dve")), scratch_busy[h]], inc="dve")
            ctx_free[h] = ("dve", P.cur("dve"))

            def doit():
                P.op("vector", lambda v: v.reciprocal(rden[h][:, :], den_sb[h][:, :]),
                     waits=[("dve", P.cur("dve")), rden_busy[h]], inc="dve")
                rd_done = P.cur("dve")
                den_sb_busy[h] = ("dve", rd_done)
                si = scp_rot.idx()
                aw = scp_rot.acquire()
                P.op("tensor", lambda t, si=si: t.matmul(
                    scp_t[si][:, 0:512], onesf[:, :], rden[h][:, :],
                    start=True, stop=True),
                    waits=[("dve", rd_done)] + aw, inc="pe")
                bc_done = P.cur("pe")
                rden_busy[h] = ("pe", bc_done)
                P.op("vector", lambda v, si=si: v.tensor_tensor(
                    out=misc[:, 4096 + h * 512: 4096 + h * 512 + 512],
                    in0=scratch[:, h * 512:(h + 1) * 512],
                    in1=scp_t[si][:, 0:512], op=ALU.mult),
                    waits=[("pe", bc_done), ("dve", P.cur("dve")),
                           norm_stage_free[h]], inc="dve")
                scp_rot.commit("dve", P.cur("dve"))
                mult_done = P.cur("dve")
                scratch_busy[h] = ("dve", mult_done)
                seg = min(512, TMY)
                cs = "cst0" if h == 0 else "cst1"
                for si2 in range(512 // seg):
                    tok0 = qc * 512 + si2 * seg
                    chunk, within = tok0 // TMY, tok0 % TMY
                    P.dma("sync", cs, lambda s, ch=chunk, wi=within, si2=si2, seg=seg: s.dma_start(
                        out=ctx_sh.ap()[ch:ch + 1, :].rearrange(
                            "r (dh t) -> (r dh) t", dh=HPC * HD)[h * 128:(h + 1) * 128, wi:wi + seg],
                        in_=misc[:, 4096 + h * 512 + si2 * seg: 4096 + h * 512 + (si2 + 1) * seg]),
                        waits=[("dve", mult_done)])
                norm_stage_free[h] = (cs, P.cur(cs))
            pending_norm.append(doit)

        pair_idx = 0
        pending_dp = [None]

        def emit_dp(h, kc, psl, p_ready):
            st_flag = (kc == 0)
            sp_flag = False
            for i in range(2):
                st = st_flag and psl.pr == 0 and i == 0
                sp = psl.sp and psl.pr == 1 and i == 1
                ktg = kc * 4 + psl.pr * 2 + i
                P.op("tensor", lambda t, h=h, pv_=psl.view, i=i, st=st, sp=sp: t.matmul(
                    denp[h][0:1, :], oneskc[:, :],
                    pv_[:, i * 512:(i + 1) * 512], start=st, stop=sp),
                    waits=[p_ready] + ([den_free[h]] if st else []),
                    inc=None)
                P.op("tensor", lambda t, h=h, pv_=psl.view, i=i, ktg=ktg, st=st, sp=sp: t.matmul(
                    ctxp[h][:, :],
                    arena[:, V0 + ktg * 256 + h * 128:
                          V0 + ktg * 256 + (h + 1) * 128],
                    pv_[:, i * 512:(i + 1) * 512], start=st, stop=sp),
                    waits=([ctx_free[h]] if st else []),
                    inc="pv" if i == 1 else None)
            p_rot.commit("pv", P.cur("pv"))

        class _Psl:
            pass

        for qc in range(NQC):
            for kc in range(qc + 1):
                for h in range(2):
                    for pr in range(2):
                        g = pair_idx
                        si = scp_rot.idx()
                        scp = scp_t[si]
                        aw = scp_rot.acquire()
                        for i in range(2):
                            kt = pr * 2 + i
                            P.op("tensor", lambda t, h=h, qc=qc, kc=kc, kt=kt, i=i, scp=scp: t.matmul(
                                scp[:, i * 512:(i + 1) * 512],
                                arena[:, K0 + h * S + kc * 512 + kt * 128:
                                      K0 + h * S + kc * 512 + (kt + 1) * 128],
                                arena[:, Q0 + h * S + qc * 512: Q0 + h * S + (qc + 1) * 512],
                                start=True, stop=True),
                                waits=(aw + [("pe", qkv_pe), ("act", qkv_act),
                                             ("dve", qkv_dve), CONS])
                                if i == 0 else [],
                                inc="sc" if i == 1 else None)
                        sc_done = P.cur("sc")
                        if pending_dp[0] is not None:
                            pending_dp[0]()
                            pending_dp[0] = None
                        pw = p_rot.acquire()
                        psl = misc[:, (g % 4) * PSL:(g % 4 + 1) * PSL]
                        P.op("scalar", lambda sc_, scp=scp, psl=psl: sc_.activation(
                            out=psl, in_=scp[:, :], func=AF.Exp, scale=SM_SCALE),
                            waits=[("sc", sc_done)] + pw, inc="exp")
                        scp_rot.commit("exp", P.cur("exp"))
                        p_ready = ("exp", P.cur("exp"))
                        if kc == qc:
                            P.op("vector", lambda v, psl=psl, pr=pr: v.tensor_tensor(
                                out=psl, in0=psl,
                                in1=maskd[:, pr * 1024:(pr + 1) * 1024], op=ALU.mult),
                                waits=[("exp", P.cur("exp")), ("dve", P.cur("dve")),
                                       CONS],
                                inc="msk")
                            p_ready = ("msk", P.cur("msk"))
                        pobj = _Psl()
                        pobj.view = psl
                        pobj.pr = pr
                        pobj.sp = (kc == qc)
                        pending_dp[0] = (lambda h=h, kc=kc, pobj=pobj, pr_=p_ready:
                                         emit_dp(h, kc, pobj, pr_))
                        pair_idx += 1
                    if kc == qc:
                        if pending_dp[0] is not None:
                            pending_dp[0]()
                            pending_dp[0] = None
                        emit_normalize(qc, h, P.cur("pv"))
                if kc < qc:
                    flush_norm(0)
        flush_norm(0)
        CST = [("cst0", P.cur("cst0")), ("cst1", P.cur("cst1"))]
        attn_dve = P.cur("dve")

        def a2a_fn(g):
            with allow_ici_all_to_all():
                return g.collective_compute(
                    "AllToAll", ALU.bypass, replica_groups=[list(range(NC_))],
                    ins=[ctx_sh.ap().opt()], outs=[ctx_a2a.ap().opt()])
        P.op("gpsimd", a2a_fn, waits=CST + CCW, inc="cca2a", inc_amt=None)
        CCA2A = ("cca2a", 1)

        attn_done = [("pv", P.cur("pv")), ("exp", P.cur("exp")),
                     ("msk", P.cur("msk"))]
        arena_tap = []
        if DEBUG_TAPS:
            arena_tap = [P.dma("gpsimd", "dbg", lambda g: g.dma_start(
                out=dbg_qkvsb.ap(), in_=arena[:, :]),
                waits=attn_done + [("dve", P.cur("dve"))])]
        for r in range(NC_):
            for dc in range(2):
                cq = ("sync", "ctxld") if r < 4 else ("scalar", "ctxld2")
                P.dma(cq[0], cq[1], lambda s, r=r, dc=dc: s.dma_start(
                    out=arena[:, (r * 2 + dc) * TMY:(r * 2 + dc + 1) * TMY],
                    in_=ctx_a2a.ap()[r:r + 1, :].rearrange(
                        "r (dh t) -> (r dh) t", dh=HPC * HD)[dc * 128:(dc + 1) * 128, :]),
                    waits=[CCA2A] + attn_done + arena_tap)
        CTXLD = ("ctxld", P.cur("ctxld"))
        CTXLD2 = ("ctxld2", P.cur("ctxld2"))

        wo_rot = Rot(2)
        op_free = {}
        res_rot = Rot(2)
        for do in range(NDO):
            sslot = wo_rot.idx()
            sview = stream[:, sslot * 8192: sslot * 8192 + NDC * 512]
            wosem = "wo0" if sslot == 0 else "wo1"
            wo_ld = P.dma("gpsimd", wosem, lambda g, do=do, sv=sview: g.dma_start(
                out=sv.rearrange("p (a b) -> p a b", b=512),
                in_=wout[:, do * 512:(do + 1) * 512].rearrange(
                    "(dc p) o -> p dc o", p=128)),
                waits=wo_rot.acquire() + [("pe", qkv_pe)])
            for tt in range(NTT):
                bank = tt % 8
                ps = pb[bank]
                aw = [op_free[bank]] if bank in op_free else []
                for dh in range(NDC):
                    P.op("tensor", lambda t, dh=dh, tt=tt, ps=ps, sv=sview: t.matmul(
                        ps, arena[:, dh * TMY + tt * 128: dh * TMY + (tt + 1) * 128],
                        sv[:, dh * 512:(dh + 1) * 512],
                        start=(dh == 0), stop=False),
                        waits=([CTXLD, CTXLD2, wo_ld] + aw) if dh == 0 else [],
                        inc=None)
                P.op("tensor", lambda t, do=do, ps=ps: t.matmul(
                    ps, onesr[:, :], bout_bf[0:1, do * 512:(do + 1) * 512],
                    start=False, stop=True),
                    waits=[GCONS], inc="pe")
                mm_done = P.cur("pe")
                rslot = res_rot.idx()
                rview = scratch[:, rslot * 512:(rslot + 1) * 512]
                rsem = "rp0" if rslot == 0 else "rp1"
                ld = P.dma("sync", rsem, lambda s, tt=tt, do=do, rv=rview: s.dma_start(
                    out=rv, in_=x_my[tt * 128:(tt + 1) * 128, do * 512:(do + 1) * 512]),
                    waits=res_rot.acquire() + [("dve", attn_dve)])
                P.op("vector", lambda v, ps=ps, rv=rview: v.tensor_tensor(
                    out=rv, in0=ps, in1=rv, op=ALU.add),
                    waits=[("pe", mm_done), ld, ("dve", P.cur("dve"))],
                    inc="dve")
                op_free[bank] = ("dve", P.cur("dve"))
                st = P.dma("sync", "x1st0" if rslot == 0 else "x1st1",
                           lambda s, tt=tt, do=do, rv=rview: s.dma_start(
                    out=x1res_d.ap()[tt * 128:(tt + 1) * 128, do * 512:(do + 1) * 512],
                    in_=rv),
                    waits=[("dve", P.cur("dve"))])
                res_rot.commit(*st)
            wo_rot.commit("pe", P.cur("pe"))
        X1ST = [("x1st0", P.cur("x1st0")), ("x1st1", P.cur("x1st1"))]

        xtr_rot = bf_rot
        xn2_stage_free = ("pe", 0)
        pending_xtr = [None]

        def flush_xtr():
            fn = pending_xtr[0]
            pending_xtr[0] = None
            if fn is not None:
                fn()

        for tt in range(NTT):
            slot = (tt % 2) * 2048
            tile_done = []
            for s_ in range(2):
                idxs = [do * NTT + tt for do in range(NDO)
                        if (do * NTT + tt) % 2 == s_]
                if idxs:
                    tile_done.append((f"x1st{s_}", 16 * (max(idxs) // 2 + 1)))
            xq2 = ("sync", "xld") if tt % 2 == 0 else ("scalar", "xld2")
            ld = P.dma(xq2[0], xq2[1], lambda s, tt=tt: s.dma_start(
                out=scratch[:, 2048:4096],
                in_=x1res_d.ap()[tt * 128:(tt + 1) * 128, :]),
                waits=tile_done + [("act", P.cur("act")), ("dve", P.cur("dve"))])
            a_done, _ = emit_ln_tile(misc[:, slot:slot + 2048], [ld],
                                     apply_waits=[xn2_stage_free], slot=1)

            def emit_xtr(tt=tt, slot=slot, a_done=a_done):
                for dc in range(NDC):
                    tview = pbbf[xtr_rot.idx()][:, 0:128]
                    aw = xtr_rot.acquire()
                    if tt == 0 and dc < 2:
                        bank = 0 if dc == 0 else 2
                        if bank in op_free:
                            aw = aw + [op_free[bank]]
                    P.op("tensor", lambda t, dc=dc, tv=tview, slot=slot: t.transpose(
                        tv, misc[:, slot + dc * 128: slot + (dc + 1) * 128],
                        ident[:, :]),
                        waits=[("act", a_done)] + aw, inc="pe")
                    tr = P.cur("pe")
                    P.op("vector", lambda v, dc=dc, tt=tt, tv=tview: v.tensor_copy(
                        arena[:, XT0 + dc * TMY + tt * 128: XT0 + dc * TMY + (tt + 1) * 128],
                        tv),
                        waits=[("pe", tr), ("dve", P.cur("dve"))], inc="dve")
                    xtr_rot.commit("dve", P.cur("dve"))
            prev = pending_xtr[0]
            pending_xtr[0] = emit_xtr
            if prev is not None:
                prev()
                xn2_stage_free = ("pe", P.cur("pe"))
        flush_xtr()
        xn2_stage_free = ("pe", P.cur("pe"))
        xn2_dve = P.cur("dve")
        xn2_pe = P.cur("pe")
        xn2_act = P.cur("act")

        w12_rot = Rot(2)
        ff_free = {}
        hstage_free = [("pe", 0), ("pe", 0)]
        sigstage_free = [("pe", 0), ("pe", 0)]
        for j in range(NJC):
            sslot = w12_rot.idx()
            w1v = stream[:, sslot * 4096: sslot * 4096 + 2048]
            w2v = stream[:, sslot * 4096 + 2048: sslot * 4096 + 4096]
            wlsem = "wl0" if sslot == 0 else "wl1"
            aw = w12_rot.acquire()
            P.dma("sync", wlsem, lambda s, j=j, wv=w1v: s.dma_start(
                out=wv.rearrange("p (a b) -> p a b", b=128),
                in_=w1t_ag.ap()[:, j * 128:(j + 1) * 128].rearrange(
                    "(dc p) j2 -> p dc j2", p=128)),
                waits=CCW[:1] + aw)
            ld = P.dma("sync", wlsem, lambda s, j=j, wv=w2v: s.dma_start(
                out=wv.rearrange("p (a b) -> p a b", b=128),
                in_=w2t_ag.ap()[:, j * 128:(j + 1) * 128].rearrange(
                    "(dc p) j2 -> p dc j2", p=128)),
                waits=CCW[1:2])
            last_pe = 0
            for tc in range(NTC):
                for wi, wv in ((0, w1v), (1, w2v)):
                    bank = (j % 2) * 4 + tc * 2 + wi
                    ps = pb[bank]
                    awf = [ff_free[bank]] if bank in ff_free else \
                        [op_free.get(bank, ("pe", 0))]
                    for dc in range(NDC):
                        P.op("tensor", lambda t, dc=dc, tc=tc, ps=ps, wv=wv: t.matmul(
                            ps[:, 0:TCH], wv[:, dc * 128:(dc + 1) * 128],
                            arena[:, XT0 + dc * TMY + tc * TCH:
                                  XT0 + dc * TMY + tc * TCH + TCH],
                            start=(dc == 0), stop=(dc == NDC - 1)),
                            waits=([ld, ("dve", xn2_dve), ("pe", xn2_pe)]
                                   + awf) if dc == 0 else [],
                            inc="pe" if dc == NDC - 1 else None)
                    last_pe = P.cur("pe")
                    par = j % 2
                    sig_off = par * 2048
                    hid_off = 4096 + par * TMY
                    if wi == 0:
                        P.op("scalar", lambda sc, ps=ps, tc=tc, so=sig_off: sc.activation(
                            out=misc[:, so + tc * TCH: so + tc * TCH + TCH],
                            in_=ps[:, 0:TCH], func=AF.Sigmoid),
                            waits=[("pe", last_pe), ("act", P.cur("act")),
                                   sigstage_free[par]], inc="act")
                        ff_free[bank] = ("act", P.cur("act"))
                    else:
                        P.op("vector", lambda v, ps=ps, tc=tc, so=sig_off, ho=hid_off: v.tensor_tensor(
                            out=misc[:, ho + tc * TCH: ho + tc * TCH + TCH],
                            in0=misc[:, so + tc * TCH: so + tc * TCH + TCH],
                            in1=ps[:, 0:TCH], op=ALU.mult),
                            waits=[("pe", last_pe), ("act", P.cur("act")),
                                   ("dve", P.cur("dve")), hstage_free[par]] + CST,
                            inc="dve")
                        ff_free[bank] = ("dve", P.cur("dve"))
                        sigstage_free[par] = ("dve", P.cur("dve"))
            hid_done = P.cur("dve")
            hsem = "hst0" if j % 2 == 0 else "hst1"
            hstage_free[j % 2] = P.dma("gpsimd", hsem, lambda g, j=j: g.dma_start(
                out=hidden_d.ap()[j * 128:(j + 1) * 128, :],
                in_=misc[:, 4096 + (j % 2) * TMY:4096 + (j % 2) * TMY + TMY]),
                waits=[("dve", hid_done)])
            w12_rot.commit("pe", last_pe)
        HST = [("hst0", P.cur("hst0")), ("hst1", P.cur("hst1"))]

        hb_rot = Rot(4)
        w3_rot = Rot(4)
        x3_free = {}
        first5b = [("act", xn2_act), ("dve", P.cur("dve"))]
        rslot_free = {tt: first5b for tt in range(NTT)}
        rsems = ["rp0", "rp1", "rp2", "rp3", "rp4", "rp5", "rp6", "rp7"]
        for do in range(NDO):
            piece_lds = {}
            for tt in range(NTT):
                rview = scratch[:, (tt % 8) * 512:((tt % 8) + 1) * 512]
                piece_lds[tt] = P.dma("sync", rsems[tt % 8],
                                      lambda s, tt=tt, do=do, rv=rview: s.dma_start(
                    out=rv, in_=x1res_d.ap()[tt * 128:(tt + 1) * 128,
                                             do * 512:(do + 1) * 512]),
                    waits=list(rslot_free[tt]) + X1ST)
            for j in range(NJC):
                hslot = hb_rot.idx()
                hview = misc[:, hslot * TMY:(hslot + 1) * TMY]
                hsem = ["hb0", "hb1", "hb2", "hb3"][hslot]
                hw = [("hst0" if j % 2 == 0 else "hst1", 16 * (j // 2 + 1))]
                ld_h = P.dma("sync", hsem, lambda s, j=j, hv=hview: s.dma_start(
                    out=hv, in_=hidden_d.ap()[j * 128:(j + 1) * 128, :]),
                    waits=hw + hb_rot.acquire())
                w3slot = w3_rot.idx()
                w3s = stream[:, 8192 + w3slot * 512: 8192 + (w3slot + 1) * 512]
                wsem = ["w30", "w31", "w32", "w33"][w3slot]
                ld_w = P.dma("scalar", wsem, lambda s, j=j, do=do, wv=w3s: s.dma_start(
                    out=wv, in_=w3t_ag.ap()[j * 128:(j + 1) * 128,
                                            do * 512:(do + 1) * 512]),
                    waits=w3_rot.acquire() + CCW[2:3])
                for tt in range(NTT):
                    bank = tt % 8
                    ps = pb[bank]
                    awf = ([x3_free[bank]] if bank in x3_free else
                           [ff_free.get(bank, ("pe", 0))]) if j == 0 else []
                    P.op("tensor", lambda t, tt=tt, ps=ps, hv=hview, wv=w3s,
                         st=(j == 0), sp=(j == NJC - 1): t.matmul(
                        ps, hv[:, tt * 128:(tt + 1) * 128], wv,
                        start=st, stop=sp),
                        waits=[ld_h, ld_w] + awf,
                        inc="pe" if tt == NTT - 1 else None)
                last = P.cur("pe")
                hb_rot.commit("pe", last)
                w3_rot.commit("pe", last)
            x3_pe = P.cur("pe")
            for tt in range(NTT):
                bank = tt % 8
                ps = pb[bank]
                rview = scratch[:, (tt % 8) * 512:((tt % 8) + 1) * 512]
                P.op("vector", lambda v, ps=ps, rv=rview: v.tensor_tensor(
                    out=rv, in0=ps, in1=rv, op=ALU.add),
                    waits=[("pe", x3_pe), piece_lds[tt], ("dve", P.cur("dve"))],
                    inc="dve")
                x3_free[bank] = ("dve", P.cur("dve"))
                st = P.dma("sync", rsems[tt % 8],
                           lambda s, tt=tt, do=do, rv=rview: s.dma_start(
                    out=y_my[tt * 128:(tt + 1) * 128, do * 512:(do + 1) * 512],
                    in_=rv),
                    waits=[("dve", P.cur("dve"))])
                rslot_free[tt] = [st]

        if DEBUG_TAPS:
            for dst, srcT, w in (
                    (dbg_ctx, ctx_a2a, [CCA2A]),
                    (dbg_ctxsh, ctx_sh, [CCA2A]),
                    (dbg_x1res, x1res_d, X1ST),
                    (dbg_hid, hidden_d, HST),
                    (dbg_w1t, w1t_ag, CCW[:1]),
                    (dbg_w3t, w3t_ag, CCW[2:3]),
                    (dbg_gam, gam_out, [("ccgam", 1)])):
                P.dma("gpsimd", "dbg", lambda g, d=dst, s2=srcT: g.dma_start(
                    out=d.ap(), in_=s2.ap()), waits=list(w))

        finals = [(s, P.cur(s)) for s in DSEMS + CSEMS if P.cur(s) > 0]
        P.op("sync", lambda s: s.nop(), waits=finals)

        with nc.Block() as block:
            @block.tensor
            def _(t):
                P.run("tensor", t, sems)

            @block.scalar
            def _(sc):
                P.run("scalar", sc, sems)

            @block.vector
            def _(v):
                P.run("vector", v, sems)

            @block.gpsimd
            def _(g):
                P.run("gpsimd", g, sems)

            @block.sync
            def _(s):
                P.run("sync", s, sems)

    return nc


def make_inputs(x, Wqkv, Wout, b_out, w1, w2, w3, S, HFF):
    TMY = S // NC_
    W12SH = D // NC_
    W3SH = HFF // NC_
    bf = ml_dtypes.bfloat16
    maskd = np.triu(np.ones((512, 512), np.float32)).astype(bf)
    ident = np.eye(128, dtype=bf)
    oneskc = np.ones((128, 1), bf)
    onesr = np.ones((1, 128), bf)
    onesf = np.ones((1, 128), np.float32)
    in_maps = []
    for c in range(NC_):
        h0, h1 = 2 * c, 2 * c + 1
        wq = np.concatenate([
            Wqkv[:, h0 * HD:(h0 + 1) * HD], Wqkv[:, h1 * HD:(h1 + 1) * HD],
            Wqkv[:, D + h0 * HD: D + (h0 + 1) * HD],
            Wqkv[:, D + h1 * HD: D + (h1 + 1) * HD],
            Wqkv[:, 2 * D + h0 * HD: 2 * D + (h0 + 1) * HD],
            Wqkv[:, 2 * D + h1 * HD: 2 * D + (h1 + 1) * HD],
        ], axis=1)
        in_maps.append({
            "x_my": np.ascontiguousarray(x[c * TMY:(c + 1) * TMY, :]),
            "wqkv_my": np.ascontiguousarray(wq),
            "wout": np.ascontiguousarray(Wout),
            "bout": np.ascontiguousarray(b_out.reshape(1, D)),
            "w1_my": np.ascontiguousarray(w1[:, c * W12SH:(c + 1) * W12SH]),
            "w2_my": np.ascontiguousarray(w2[:, c * W12SH:(c + 1) * W12SH]),
            "w3_my": np.ascontiguousarray(w3[:, c * W3SH:(c + 1) * W3SH]),
            "ident": ident, "maskd": maskd, "oneskc": oneskc,
            "onesr": onesr, "onesf": onesf,
        })
    return in_maps


def run(x, Wqkv, Wout, b_out, w1, w2, w3, S, HFF, full_results=False):
    in_maps = make_inputs(x, Wqkv, Wout, b_out, w1, w2, w3, S, HFF)
    nc = build_program(S, HFF)
    res = run_bass_kernel_spmd(nc, in_maps, core_ids=list(range(NC_)))
    y = np.concatenate([res.results[c]["y_my"] for c in range(NC_)], axis=0)
    if full_results:
        return y, res.results
    return y


def kernel(L0_attn_x, L0_attn_W_qkv, L0_attn_W_out, L0_attn_b_out,
           L0_ffn_w1, L0_ffn_w2, L0_ffn_w3):
    S, HFF = 8192, 5632
    y = run(np.asarray(L0_attn_x, np.float32).reshape(S, D),
            np.asarray(L0_attn_W_qkv, np.float32),
            np.asarray(L0_attn_W_out, np.float32),
            np.asarray(L0_attn_b_out, np.float32),
            np.asarray(L0_ffn_w1, np.float32),
            np.asarray(L0_ffn_w2, np.float32),
            np.asarray(L0_ffn_w3, np.float32),
            S, HFF)
    return y.reshape(1, S, D).astype(np.float32)


# version 20
# speedup vs baseline: 1.0147x; 1.0147x over previous
import numpy as np
import ml_dtypes
from contextlib import ExitStack

import concourse.bass as bass
import concourse.mybir as mybir
from concourse.bass_utils import run_bass_kernel_spmd
from concourse.aot_env import allow_ici_all_to_all

F32 = mybir.dt.float32
BF16 = mybir.dt.bfloat16
AF = mybir.ActivationFunctionType
ALU = mybir.AluOpType
AX = mybir.AxisListType

NC_ = 8
D = 2048
NDC = D // 128
HD = 128
HPC = 2
EPS_LN = 1e-5
EPS_Q = 1e-8
SM_SCALE = float(1.0 / np.sqrt(HD))
DEBUG_TAPS = False


class Plan:
    ENGINES = ("tensor", "scalar", "vector", "gpsimd", "sync")

    def __init__(self):
        self.plan = {e: [] for e in self.ENGINES}
        self.count = {}
        self.last_wait = {}
        self.semnames = []

    def sem(self, name):
        if name not in self.count:
            self.count[name] = 0
            self.semnames.append(name)
        return name

    def inc(self, name, amt):
        self.count[name] = self.count.get(name, 0) + amt
        return self.count[name]

    def cur(self, name):
        return self.count.get(name, 0)

    def op(self, eng, fn, waits=(), inc=None, inc_amt=1):
        w = []
        for s, v in waits:
            if v <= 0:
                continue
            k = (eng, s)
            if self.last_wait.get(k, -1) >= v:
                continue
            self.last_wait[k] = v
            w.append((s, v))
        if inc is not None:
            if inc_amt is None:
                self.inc(inc, 1)
            else:
                self.inc(inc, inc_amt)
        self.plan[eng].append((w, fn, inc, inc_amt))

    def dma(self, eng, stream, fn, waits=()):
        self.op(eng, fn, waits=waits, inc=stream, inc_amt=16)
        return (stream, self.cur(stream))

    def run(self, eng, engobj, sems):
        for w, fn, inc, inc_amt in self.plan[eng]:
            for s, v in w:
                engobj.wait_ge(sems[s], v)
            inst = fn(engobj)
            if inc is not None:
                if inc_amt is None:
                    inst.then_inc(sems[inc])
                else:
                    inst.then_inc(sems[inc], inc_amt)


class Rot:

    def __init__(self, depth):
        self.depth = depth
        self.users = []

    def idx(self):
        return len(self.users) % self.depth

    def acquire(self):
        i = len(self.users)
        if i < self.depth:
            return []
        return [self.users[i - self.depth]]

    def commit(self, sem, val):
        self.users.append((sem, val))


def build_program(S, HFF):
    TMY = S // NC_
    NTT = TMY // 128
    TCH = min(512, TMY)
    NTC = TMY // TCH
    NQC = S // 512
    NJC = HFF // 128
    W12SH = D // NC_
    W3SH = HFF // NC_
    NDO = D // 512
    CH = HPC * HD * TMY

    nc = bass.Bass("TRN2", target_bir_lowering=False, debug=False)

    x_my = nc.dram_tensor("x_my", [TMY, D], F32, kind="ExternalInput").ap()
    wqkv_my = nc.dram_tensor("wqkv_my", [D, 6 * HD], F32, kind="ExternalInput").ap()
    wout = nc.dram_tensor("wout", [D, D], F32, kind="ExternalInput").ap()
    bout = nc.dram_tensor("bout", [1, D], F32, kind="ExternalInput").ap()
    w1_my = nc.dram_tensor("w1_my", [HFF, W12SH], F32, kind="ExternalInput").ap()
    w2_my = nc.dram_tensor("w2_my", [HFF, W12SH], F32, kind="ExternalInput").ap()
    w3_my = nc.dram_tensor("w3_my", [D, W3SH], F32, kind="ExternalInput").ap()
    ident_in = nc.dram_tensor("ident", [128, 128], BF16, kind="ExternalInput").ap()
    mask_in = nc.dram_tensor("maskd", [512, 512], BF16, kind="ExternalInput").ap()
    oneskc_in = nc.dram_tensor("oneskc", [128, 1], BF16, kind="ExternalInput").ap()
    onesr_in = nc.dram_tensor("onesr", [1, 128], BF16, kind="ExternalInput").ap()
    onesf_in = nc.dram_tensor("onesf", [1, 128], F32, kind="ExternalInput").ap()

    y_my = nc.dram_tensor("y_my", [TMY, D], F32, kind="ExternalOutput").ap()

    NPART = 4
    THP = TMY // NPART
    PPC = NPART // NTC
    xn_sh = [nc.dram_tensor(f"xn_sh{i}", [D, THP], BF16) for i in range(NPART)]
    xn_ag = [nc.dram_tensor(f"xn_ag{i}", [NC_ * D, THP], BF16,
                            addr_space="Shared") for i in range(NPART)]
    gam_in = nc.dram_tensor("gam_in", [1, 4], F32)
    gam_out = nc.dram_tensor("gam_out", [1, 4], F32)
    w1t_sh = nc.dram_tensor("w1t_sh", [W12SH, HFF], BF16)
    w1t_ag = nc.dram_tensor("w1t_ag", [D, HFF], BF16, addr_space="Shared")
    w2t_sh = nc.dram_tensor("w2t_sh", [W12SH, HFF], BF16)
    w2t_ag = nc.dram_tensor("w2t_ag", [D, HFF], BF16, addr_space="Shared")
    w3t_sh = nc.dram_tensor("w3t_sh", [W3SH, D], BF16)
    w3t_ag = nc.dram_tensor("w3t_ag", [HFF, D], BF16, addr_space="Shared")
    ctx_sh = nc.dram_tensor("ctx_sh", [NC_, CH], BF16)
    ctx_a2a = nc.dram_tensor("ctx_a2a", [NC_, CH], BF16)
    x1res_d = nc.dram_tensor("x1res_d", [TMY, D], F32)
    hidden_d = nc.dram_tensor("hidden_d", [HFF, TMY], BF16)
    if DEBUG_TAPS:
        dbg_ctx = nc.dram_tensor("dbg_ctx", [NC_, CH], BF16, kind="ExternalOutput")
        dbg_ctxsh = nc.dram_tensor("dbg_ctxsh", [NC_, CH], BF16, kind="ExternalOutput")
        dbg_x1res = nc.dram_tensor("dbg_x1res", [TMY, D], F32, kind="ExternalOutput")
        dbg_hid = nc.dram_tensor("dbg_hid", [HFF, TMY], BF16, kind="ExternalOutput")
        dbg_w1t = nc.dram_tensor("dbg_w1t", [D, HFF], BF16, kind="ExternalOutput")
        dbg_w3t = nc.dram_tensor("dbg_w3t", [HFF, D], BF16, kind="ExternalOutput")
        dbg_gam = nc.dram_tensor("dbg_gam", [1, 4], F32, kind="ExternalOutput")
        dbg_qkvsb = nc.dram_tensor("dbg_qkvsb", [128, 6 * S], BF16, kind="ExternalOutput")

    P = Plan()
    for s in ("sc", "pv", "exp", "msk", "act", "dve", "pe", "pool"):
        P.sem(s)
    DSEMS = ["con", "gcon", "xld", "xnst", "gaml", "gamd", "qld",
             "xb0s", "xb0a", "xb1s", "xb1a",
             "cst0", "cst1", "ctxld", "rp0", "rp1", "x1st0", "x1st1",
             "wl0", "wl1", "hst0", "hst1", "hb0", "hb1", "hb2", "hb3",
             "w30", "w31", "w32", "w33",
             "yst0", "yst1", "qst", "wo0", "wo1", "dbg",
             "rp2", "rp3", "rp4", "rp5", "rp6", "rp7", "xnt0", "xnt1",
             "gamlb", "xld2"]
    CSEMS = ["ccxn", "ccgam", "ccw1", "ccw2", "ccw3", "cca2a"]
    for s in DSEMS + CSEMS:
        P.sem(s)

    with ExitStack() as ctx:
        E = ctx.enter_context
        arena = E(nc.sbuf_tensor("arena", [128, 6 * S], BF16))
        Q0, K0, V0 = 0, 2 * S, 4 * S
        XT0 = 2 * S
        stream = E(nc.sbuf_tensor("stream", [128, 16384], BF16))
        misc = E(nc.sbuf_tensor("misc", [128, 8192], BF16))
        scratch = E(nc.sbuf_tensor("scratch", [128, 4096], F32))
        wqkv_sb = E(nc.sbuf_tensor("wqkv_sb", [128, NDC * 6 * HD], BF16))
        maskd = E(nc.sbuf_tensor("maskd_sb", [128, 2048], BF16))
        ident = E(nc.sbuf_tensor("ident_sb", [128, 128], BF16))
        oneskc = E(nc.sbuf_tensor("oneskc_sb", [128, 1], BF16))
        onesr = E(nc.sbuf_tensor("onesr_sb", [1, 128], BF16))
        onesf = E(nc.sbuf_tensor("onesf_sb", [1, 128], F32))
        bout_bf = E(nc.sbuf_tensor("bout_bf", [1, D], BF16))
        lnf = E(nc.sbuf_tensor("lnf", [128, 32], F32))
        den_sb = [E(nc.sbuf_tensor(f"den_sb{h}", [1, 512], F32)) for h in range(2)]
        rden = [E(nc.sbuf_tensor(f"rden{h}", [1, 512], F32)) for h in range(2)]
        gamr = E(nc.sbuf_tensor("gamr", [1, 4], F32))
        scpA = E(nc.psum_tensor("scpA", [128, 1024], F32))
        scpB = E(nc.psum_tensor("scpB", [128, 1024], F32))
        ctxp = [E(nc.psum_tensor(f"ctxp{h}", [128, 512], F32)) for h in range(2)]
        denp = [E(nc.psum_tensor(f"denp{h}", [128, 512], F32)) for h in range(2)]
        sems = {s: E(nc.semaphore(s)) for s in P.semnames}

        scp_t = [scpA, scpB]
        pb = [scpA[:, 0:512], scpA[:, 512:1024], scpB[:, 0:512], scpB[:, 512:1024],
              ctxp[0][:, :], ctxp[1][:, :], denp[0][:, :], denp[1][:, :]]
        pbbf = [scpA[:, 0:512].bitcast(BF16), scpB[:, 0:512].bitcast(BF16)]

        P.dma("sync", "con", lambda s: s.dma_start(out=ident[:, :], in_=ident_in))
        P.dma("sync", "con", lambda s: s.dma_start(
            out=maskd[:, :].rearrange("p (kt q) -> p kt q", kt=4),
            in_=mask_in.rearrange("(kt p) q -> p kt q", p=128)))
        P.dma("sync", "con", lambda s: s.dma_start(out=oneskc[:, :], in_=oneskc_in))
        P.dma("sync", "con", lambda s: s.dma_start(out=onesr[:, :], in_=onesr_in))
        P.dma("sync", "con", lambda s: s.dma_start(out=onesf[:, :], in_=onesf_in))
        P.dma("gpsimd", "gcon", lambda g: g.dma_start(out=bout_bf[:, :], in_=bout))
        P.dma("gpsimd", "gcon", lambda g: g.dma_start(
            out=wqkv_sb[:, :].rearrange("p (dc o) -> p dc o", dc=NDC),
            in_=wqkv_my.rearrange("(dc p) o -> p dc o", p=128)))
        CONS = ("con", P.cur("con"))
        GCONS = ("gcon", P.cur("gcon"))

        def emit_ln_tile(out_bf_tile, ld_waits, apply_waits=(), slot=0):
            src = scratch[:, slot * 2048:(slot + 1) * 2048]
            cb = slot * 16
            c = lambda i: lnf[:, cb + i:cb + i + 1]
            P.op("vector", lambda v: v.reduce_sum(c(0), src, axis=AX.X),
                 waits=list(ld_waits) + [("dve", P.cur("dve"))], inc="dve")
            P.op("scalar", lambda sc, o=out_bf_tile: sc.activation(
                out=o, in_=src, func=AF.Square,
                accum_out=c(1)), waits=list(ld_waits) + list(apply_waits),
                inc="act")
            act_sq = P.cur("act")
            P.op("vector", lambda v: v.tensor_scalar(
                out=c(2), in0=c(0), scalar1=1.0 / D, scalar2=None,
                op0=ALU.mult), waits=[("dve", P.cur("dve"))], inc="dve")
            P.op("vector", lambda v: v.tensor_tensor(
                out=c(3), in0=c(2), in1=c(2), op=ALU.mult),
                waits=[("dve", P.cur("dve"))], inc="dve")
            P.op("vector", lambda v: v.scalar_tensor_tensor(
                out=c(4), in0=c(1), scalar=1.0 / D, in1=c(3),
                op0=ALU.mult, op1=ALU.subtract),
                waits=[("act", act_sq), ("dve", P.cur("dve"))], inc="dve")
            P.op("vector", lambda v: v.tensor_scalar(
                out=c(4), in0=c(4), scalar1=EPS_LN, scalar2=None,
                op0=ALU.add), waits=[("dve", P.cur("dve"))], inc="dve")
            P.op("scalar", lambda sc: sc.activation(
                out=c(5), in_=c(4), func=AF.Sqrt),
                waits=[("dve", P.cur("dve"))], inc="act")
            P.op("vector", lambda v: v.reciprocal(c(6), c(5)),
                 waits=[("act", P.cur("act")), ("dve", P.cur("dve"))], inc="dve")
            P.op("vector", lambda v: v.scalar_tensor_tensor(
                out=c(7), in0=c(2), scalar=-1.0, in1=c(6),
                op0=ALU.mult, op1=ALU.mult),
                waits=[("dve", P.cur("dve"))], inc="dve")
            P.op("scalar", lambda sc, o=out_bf_tile: sc.activation(
                out=o, in_=src, func=AF.Identity, scale=c(6), bias=c(7)),
                waits=[("dve", P.cur("dve"))], inc="act")
            return P.cur("act"), P.cur("dve")

        w1flat = w1_my.rearrange("(g p) d -> p g d", p=128)
        w2flat = w2_my.rearrange("(g p) d -> p g d", p=128)
        w3flat = w3_my.rearrange("(dc p) j -> p dc j", p=128)
        G12 = HFF // 128

        def emit_gamma(widx, src_ap, nblocks, bw, first):
            b0 = 0
            while b0 < nblocks:
                nb = min(max(1, 2048 // bw), nblocks - b0)
                cw = nb * bw
                ld = P.dma("scalar", "gaml", lambda s, a=src_ap, b0=b0, nb=nb, bw=bw, cw=cw: s.dma_start(
                    out=scratch[:, 0:cw].rearrange("p (a b) -> p a b", b=bw),
                    in_=a[:, b0:b0 + nb, :]),
                    waits=[("dve", P.cur("dve")), ("act", P.cur("act"))])
                P.op("vector", lambda v, cw=cw: v.reduce_sum(
                    lnf[:, 11:12], scratch[:, 0:cw], axis=AX.X,
                    apply_absolute_value=True),
                    waits=[ld, ("dve", P.cur("dve"))], inc="dve")
                if first and b0 == 0:
                    P.op("vector", lambda v, w=widx: v.tensor_copy(
                        lnf[:, 8 + w:9 + w], lnf[:, 11:12]),
                        waits=[("dve", P.cur("dve"))], inc="dve")
                else:
                    P.op("vector", lambda v, w=widx: v.tensor_tensor(
                        out=lnf[:, 8 + w:9 + w], in0=lnf[:, 8 + w:9 + w],
                        in1=lnf[:, 11:12], op=ALU.add),
                        waits=[("dve", P.cur("dve"))], inc="dve")
                b0 += nb

        gamma_units = []

        def gamma_unit_blocks(widx, src_ap, nblocks, bw, first):
            b0 = 0
            while b0 < nblocks:
                nb = min(max(1, 2048 // bw), nblocks - b0)
                gamma_units.append(
                    lambda widx=widx, a=src_ap, b0=b0, nb=nb, bw=bw,
                    fi=(first and b0 == 0): emit_gamma_blk(widx, a, b0, nb, bw, fi))
                b0 += nb

        gstg = stream[:, 8192:12288].bitcast(F32)

        gam_q = [0]

        def emit_gamma_blk(widx, src_ap, b0, nb, bw, first):
            cw = nb * bw
            gq = ("scalar", "gaml") if gam_q[0] % 2 == 0 else ("sync", "gamlb")
            gam_q[0] += 1
            ld = P.dma(gq[0], gq[1], lambda s, a=src_ap, b0=b0, nb=nb, bw=bw, cw=cw: s.dma_start(
                out=gstg[:, 0:cw].rearrange("p (a b) -> p a b", b=bw),
                in_=a[:, b0:b0 + nb, :]),
                waits=[("dve", P.cur("dve")), ("act", P.cur("act"))])
            P.op("vector", lambda v, cw=cw: v.reduce_sum(
                lnf[:, 11:12], gstg[:, 0:cw], axis=AX.X,
                apply_absolute_value=True),
                waits=[ld, ("dve", P.cur("dve"))], inc="dve")
            if first:
                P.op("vector", lambda v, w=widx: v.tensor_copy(
                    lnf[:, 8 + w:9 + w], lnf[:, 11:12]),
                    waits=[("dve", P.cur("dve"))], inc="dve")
            else:
                P.op("vector", lambda v, w=widx: v.tensor_tensor(
                    out=lnf[:, 8 + w:9 + w], in0=lnf[:, 8 + w:9 + w],
                    in1=lnf[:, 11:12], op=ALU.add),
                    waits=[("dve", P.cur("dve"))], inc="dve")

        gamma_unit_blocks(0, w1flat, G12, W12SH, True)
        gamma_unit_blocks(1, w2flat, G12, W12SH, True)
        gamma_unit_blocks(2, w3flat, NDC, W3SH, True)

        def emit_gamma_tail():
            P.op("vector", lambda v: v.memset(gamr[0:1, :], 0.0),
                 waits=[("dve", P.cur("dve"))], inc="dve")
            P.op("gpsimd", lambda g: g.tensor_reduce(
                out=gamr[0:1, 0:3], in_=lnf[:, 8:11], axis=AX.C, op=ALU.add),
                waits=[("dve", P.cur("dve"))], inc="pool")
            g1 = P.dma("gpsimd", "gamd", lambda g: g.dma_start(
                out=gam_in.ap(), in_=gamr[0:1, :]),
                waits=[("pool", P.cur("pool"))])
            P.op("gpsimd", lambda g: g.collective_compute(
                "AllReduce", ALU.add, replica_groups=[list(range(NC_))],
                ins=[gam_in.ap().opt()], outs=[gam_out.ap().opt()]),
                waits=[g1], inc="ccgam", inc_amt=None)
            g2 = P.dma("gpsimd", "gamd", lambda g: g.dma_start(
                out=gamr[0:1, :], in_=gam_out.ap()),
                waits=[("ccgam", 1)])
            P.op("vector", lambda v: v.tensor_scalar(
                out=gamr[0:1, :], in0=gamr[0:1, :], scalar1=1.0 / (HFF * D * 2),
                scalar2=EPS_Q / 2, op0=ALU.mult, op1=ALU.add),
                waits=[g2, ("dve", P.cur("dve"))], inc="dve")
            thr_dve = P.cur("dve")
            P.op("tensor", lambda t: t.matmul(
                denp[0][:, 0:4], onesf[:, :], gamr[0:1, :], start=True, stop=True),
                waits=[("dve", thr_dve), CONS], inc="pe")
            P.op("vector", lambda v: v.tensor_copy(lnf[:, 12:16], denp[0][:, 0:4]),
                 waits=[("pe", P.cur("pe")), ("dve", P.cur("dve"))], inc="dve")
            P.op("vector", lambda v: v.tensor_scalar(
                out=lnf[:, 8:11], in0=lnf[:, 12:15], scalar1=-1.0, scalar2=None,
                op0=ALU.mult), waits=[("dve", P.cur("dve"))], inc="dve")
            return ("dve", P.cur("dve"))
        thr_consumed = [("dve", 0)]

        TPP = THP // 128
        ccxn_n = [0]
        bf_rot = Rot(2)
        ln_tr_rot = bf_rot
        xstg_free = [("pe", 0), ("pe", 0)]
        xn_stage_free = ("pe", 0)
        pending_lntr = [None]
        xnt_sems = ["xnt0", "xnt1"]

        def flush_lntr():
            fn = pending_lntr[0]
            pending_lntr[0] = None
            if fn is not None:
                fn()

        def emit_lntr(tt, slot, a_done):
            stg = 4096 + slot * 2048
            for dc in range(NDC):
                tview = pbbf[ln_tr_rot.idx()][:, 0:128]
                aw = ln_tr_rot.acquire()
                P.op("tensor", lambda t, dc=dc, tv=tview, slot=slot: t.transpose(
                    tv, misc[:, slot * 2048 + dc * 128: slot * 2048 + (dc + 1) * 128],
                    ident[:, :]),
                    waits=[("act", a_done), CONS] + aw, inc="pe")
                tr = P.cur("pe")
                P.op("vector", lambda v, dc=dc, tv=tview, stg=stg: v.tensor_copy(
                    misc[:, stg + dc * 128: stg + (dc + 1) * 128], tv),
                    waits=[("pe", tr), ("dve", P.cur("dve")), xstg_free[slot]],
                    inc="dve")
                ln_tr_rot.commit("dve", P.cur("dve"))
            cp_done = P.cur("dve")
            part, off = tt // TPP, tt % TPP
            st = P.dma("sync", xnt_sems[slot], lambda s, p=part, o=off, stg=stg: s.dma_start(
                out=xn_sh[p].ap()[:, o * 128:(o + 1) * 128].rearrange(
                    "(dc p2) t -> p2 dc t", p2=128),
                in_=misc[:, stg:stg + 2048].rearrange("p (dc t) -> p dc t", dc=NDC)),
                waits=[("dve", cp_done)])
            xstg_free[slot] = st
            if off == TPP - 1:
                P.op("gpsimd", lambda g, p=part: g.collective_compute(
                    "AllGather", ALU.bypass, replica_groups=[list(range(NC_))],
                    ins=[xn_sh[p].ap().opt()], outs=[xn_ag[p].ap().opt()]),
                    waits=[("xnt0", P.cur("xnt0")), ("xnt1", P.cur("xnt1"))],
                    inc="ccxn", inc_amt=None)
                ccxn_n[0] += 1

        xslot_free = [("pe", 0), ("pe", 0)]
        for tt in range(NTT):
            slot = tt % 2
            xq = ("sync", "xld") if slot == 0 else ("scalar", "xld2")
            ld = P.dma(xq[0], xq[1], lambda s, tt=tt, slot=slot: s.dma_start(
                out=scratch[:, slot * 2048:(slot + 1) * 2048],
                in_=x_my[tt * 128:(tt + 1) * 128, :]),
                waits=[xslot_free[slot]])
            a_done, d_done = emit_ln_tile(
                misc[:, slot * 2048:(slot + 1) * 2048], [ld],
                apply_waits=[xn_stage_free], slot=slot)
            xslot_free[slot] = ("act", a_done)
            prev = pending_lntr[0]
            pending_lntr[0] = (lambda tt=tt, slot=slot, a=a_done: emit_lntr(tt, slot, a))
            if prev is not None:
                prev()
                xn_stage_free = ("pe", P.cur("pe"))
            ng_ = len(gamma_units)
            for u in gamma_units[(tt * ng_) // NTT:((tt + 1) * ng_) // NTT]:
                u()
        flush_lntr()
        CCXN = ("ccxn", ccxn_n[0])


        qkv_rot = Rot(2)
        vtr_rot = bf_rot
        sbuf_rot = Rot(2)
        SW = NDC * TCH
        vtmp_free = [("pe", 0), ("pe", 0)]
        pending_vtr = [None]

        def qkv_chunk(r, tc, extra=()):
            base_tok = r * TMY + tc * TCH
            sslot = sbuf_rot.idx()
            sview = stream[:, sslot * SW:(sslot + 1) * SW]
            aw = sbuf_rot.acquire()
            xbsem = "xb0s" if sslot == 0 else "xb1s"
            for sp in range(PPC):
                part = tc * PPC + sp
                P.dma("sync", xbsem, lambda e, r=r, sv=sview, sp=sp, part=part: e.dma_start(
                    out=sv.rearrange("p (dc t) -> p dc t", dc=NDC)[
                        :, :, sp * THP:(sp + 1) * THP],
                    in_=xn_ag[part].ap()[r * D:(r + 1) * D, :].rearrange(
                        "(dc p) t -> p dc t", p=128)),
                    waits=[("ccxn", part + 1)] +
                    (aw + list(extra) if sp == 0 else []))
            ld2 = [(xbsem, P.cur(xbsem))]
            last_pe = 0
            for o in range(6):
                slot = qkv_rot.idx()
                ps = pb[4 + slot]
                aw2 = qkv_rot.acquire()
                for dc in range(NDC):
                    P.op("tensor", lambda t, dc=dc, o=o, ps=ps, sv=sview: t.matmul(
                        ps[:, 0:TCH],
                        wqkv_sb[:, dc * 768 + o * 128: dc * 768 + (o + 1) * 128],
                        sv[:, dc * TCH:(dc + 1) * TCH],
                        start=(dc == 0), stop=(dc == NDC - 1)),
                        waits=(ld2 + [GCONS, CONS] + aw2) if dc == 0 else [],
                        inc="pe" if dc == NDC - 1 else None)
                mm_done = P.cur("pe")
                last_pe = mm_done
                if o < 4:
                    h = o % 2
                    dst0 = (Q0 if o < 2 else K0) + h * S + base_tok
                    P.op("scalar", lambda sc, ps=ps, d0=dst0: sc.activation(
                        out=arena[:, d0:d0 + TCH], in_=ps[:, 0:TCH], func=AF.Copy),
                        waits=[("pe", mm_done)], inc="act")
                    qkv_rot.commit("act", P.cur("act"))
                else:
                    h = o - 4
                    vtmp = misc[:, h * TCH:(h + 1) * TCH]
                    if pending_vtr[0] is not None:
                        pending_vtr[0]()
                        pending_vtr[0] = None
                    P.op("vector", lambda v, ps=ps, vt=vtmp: v.tensor_copy(vt, ps[:, 0:TCH]),
                         waits=[("pe", mm_done), ("dve", P.cur("dve")),
                                vtmp_free[h]],
                         inc="dve")
                    ev_done = P.cur("dve")
                    qkv_rot.commit("dve", ev_done)

                    def emit_vtr(h=h, vtmp=vtmp, ev_done=ev_done, base_tok=base_tok):
                        for sub in range(TCH // 128):
                            ktg = (base_tok + sub * 128) // 128
                            tview = pbbf[vtr_rot.idx()][:, 0:128]
                            P.op("tensor", lambda t, sub=sub, tv=tview, vt=vtmp: t.transpose(
                                tv, vt[:, sub * 128:(sub + 1) * 128], ident[:, :]),
                                waits=[("dve", ev_done)] + vtr_rot.acquire(), inc="pe")
                            tr_done = P.cur("pe")
                            P.op("vector", lambda v, tv=tview, ktg=ktg, h=h: v.tensor_copy(
                                arena[:, V0 + ktg * 256 + h * 128: V0 + ktg * 256 + (h + 1) * 128],
                                tv),
                                waits=[("pe", tr_done), ("dve", P.cur("dve"))], inc="dve")
                            vtr_rot.commit("dve", P.cur("dve"))
                        vtmp_free[h] = ("pe", P.cur("pe"))
                    pending_vtr[0] = emit_vtr
                    last_pe = P.cur("pe")
            sbuf_rot.commit("pe", last_pe)

        qz_read_free = [("xnt0", P.cur("xnt0")), ("xnt1", P.cur("xnt1"))]
        stage_free = [[("xnt0", P.cur("xnt0")), ("xnt1", P.cur("xnt1"))]]

        def emit_quant_tile(widx, nco, ld):
            f32_view = scratch[:, 0:nco]
            q1 = scratch[:, 2048:2048 + nco]
            q2 = misc[:, 4096:4096 + nco]
            P.op("vector", lambda v, a=f32_view, q1=q1, w=widx: v.tensor_scalar(
                out=q1, in0=a, scalar1=lnf[:, 12 + w:13 + w], scalar2=None,
                op0=ALU.is_ge),
                waits=[ld, ("dve", max(thr_consumed[0][1], P.cur("dve")))],
                inc="dve")
            P.op("vector", lambda v, a=f32_view, q2=q2, w=widx: v.tensor_scalar(
                out=q2, in0=a, scalar1=lnf[:, 8 + w:9 + w], scalar2=None,
                op0=ALU.is_le),
                waits=[("dve", P.cur("dve"))] + list(qz_read_free), inc="dve")
            P.op("vector", lambda v, q1=q1, q2=q2: v.tensor_tensor(
                out=q2, in0=q1, in1=q2, op=ALU.subtract),
                waits=[("dve", P.cur("dve"))], inc="dve")
            return P.cur("dve")

        def quant_unit_w12(widx, flat_ap, dst_sh, g0, nb):
            nco = nb * W12SH
            ld = P.dma("scalar", "qld", lambda s, a=flat_ap, g0=g0, nb=nb: s.dma_start(
                out=scratch[:, 0:nb * W12SH].rearrange("p (a b) -> p a b", b=W12SH),
                in_=a[:, g0:g0 + nb, :]),
                waits=[("act", P.cur("act")), ("dve", P.cur("dve"))])
            qz_pool = emit_quant_tile(widx, nco, ld)
            return lambda: quant_tr_w12(dst_sh, g0, nb, qz_pool)

        def quant_tr_w12(dst_sh, g0, nb, qz_pool):
            copies = []
            for dh in range(2):
                for gi in range(nb):
                    tview = pbbf[vtr_rot.idx()][:, 0:128]
                    src = misc[:, 4096 + gi * W12SH + dh * 128:
                               4096 + gi * W12SH + (dh + 1) * 128]
                    P.op("tensor", lambda t, tv=tview, s=src: t.transpose(
                        tv, s, ident[:, :]),
                        waits=[("dve", qz_pool)] + vtr_rot.acquire(), inc="pe")
                    tr = P.cur("pe")
                    P.op("vector", lambda v, tv=tview, dh=dh, gi=gi: v.tensor_copy(
                        misc[:, 6144 + dh * 1024 + gi * 128:
                             6144 + dh * 1024 + (gi + 1) * 128], tv),
                        waits=[("pe", tr), ("dve", P.cur("dve"))] +
                        (stage_free[0] if isinstance(stage_free[0], list)
                         else [stage_free[0]]),
                        inc="dve")
                    vtr_rot.commit("dve", P.cur("dve"))
                copies.append(P.cur("dve"))
            qz_read_free[:] = [("pe", P.cur("pe"))]
            for dh in range(2):
                P.dma("gpsimd", "qst", lambda g, dh=dh, g0=g0, nb=nb, d=dst_sh: g.dma_start(
                    out=d.ap()[dh * 128:(dh + 1) * 128, g0 * 128:(g0 + nb) * 128],
                    in_=misc[:, 6144 + dh * 1024: 6144 + dh * 1024 + nb * 128]),
                    waits=[("dve", copies[dh])])
            stage_free[0] = ("qst", P.cur("qst"))

        def quant_unit_w3(dcp):
            nco = 2 * W3SH
            ld = P.dma("scalar", "qld", lambda s, dcp=dcp: s.dma_start(
                out=scratch[:, 0:2 * W3SH].rearrange("p (a b) -> p a b", b=W3SH),
                in_=w3flat[:, dcp * 2:dcp * 2 + 2, :]),
                waits=[("act", P.cur("act")), ("dve", P.cur("dve"))])
            qz_pool = emit_quant_tile(2, nco, ld)
            return lambda: quant_tr_w3(dcp, qz_pool)

        def quant_tr_w3(dcp, qz_pool):
            njch = (W3SH + 127) // 128
            last_pe = 0
            for dci in range(2):
                dc = dcp * 2 + dci
                for jc in range(njch):
                    jw = min(128, W3SH - jc * 128)
                    tview = pbbf[vtr_rot.idx()][0:jw, 0:128]
                    src = misc[:, 4096 + dci * W3SH + jc * 128:
                               4096 + dci * W3SH + jc * 128 + jw]
                    P.op("tensor", lambda t, tv=tview, s=src: t.transpose(
                        tv, s, ident[:, :]),
                        waits=[("dve", qz_pool)] + vtr_rot.acquire(), inc="pe")
                    last_pe = P.cur("pe")
                    P.op("vector", lambda v, tv=tview, jw=jw: v.tensor_copy(
                        misc[0:jw, 6144:6272], tv),
                        waits=[("pe", last_pe), ("dve", P.cur("dve"))] +
                        (stage_free[0] if isinstance(stage_free[0], list)
                         else [stage_free[0]]),
                        inc="dve")
                    vtr_rot.commit("dve", P.cur("dve"))
                    P.dma("gpsimd", "qst", lambda g, dc=dc, jc=jc, jw=jw: g.dma_start(
                        out=w3t_sh.ap()[jc * 128: jc * 128 + jw, dc * 128:(dc + 1) * 128],
                        in_=misc[0:jw, 6144:6272]),
                        waits=[("dve", P.cur("dve"))])
                    stage_free[0] = [("qst", P.cur("qst"))]
            qz_read_free[:] = [("pe", last_pe)]

        quant_units = []
        for widx, (flat, dst) in enumerate(((w1flat, w1t_sh), (w2flat, w2t_sh))):
            g0 = 0
            while g0 < G12:
                nb = min(8, G12 - g0)
                quant_units.append(
                    lambda widx=widx, flat=flat, dst=dst, g0=g0, nb=nb:
                    quant_unit_w12(widx, flat, dst, g0, nb))
                g0 += nb
        for dcp in range(NDC // 2):
            quant_units.append(lambda dcp=dcp: quant_unit_w3(dcp))

        chunks = [(r, tc) for tc in range(NTC) for r in range(NC_)]
        nchunks = len(chunks)
        sched = {}
        sched.setdefault(0, []).append("TAIL")
        nq = len(quant_units)
        q0 = min(3, nchunks - 2)
        span = max(1, nchunks - q0)
        for i, u in enumerate(quant_units):
            sched.setdefault(q0 + (i * span) // max(1, nq), []).append(u)
        pending_qtr = []

        def run_units(us):
            for u in us:
                if u == "TAIL":
                    thr_consumed[0] = emit_gamma_tail()
                else:
                    if pending_qtr:
                        pending_qtr.pop(0)()
                    r = u()
                    if callable(r):
                        pending_qtr.append(r)
        gam_guard = [("dve", P.cur("dve"))]
        for i, (r, tc) in enumerate(chunks):
            qkv_chunk(r, tc, extra=gam_guard if i == 1 else ())
            run_units(sched.pop(i, []))
        for key in sorted(sched):
            run_units(sched[key])
        while pending_qtr:
            pending_qtr.pop(0)()
        if pending_vtr[0] is not None:
            pending_vtr[0]()
            pending_vtr[0] = None
        qkv_pe = P.cur("pe")
        qkv_act = P.cur("act")
        qkv_dve = P.cur("dve")
        QSTF = ("qst", P.cur("qst"))
        for (sh, ag, cs) in ((w1t_sh, w1t_ag, "ccw1"), (w2t_sh, w2t_ag, "ccw2"),
                             (w3t_sh, w3t_ag, "ccw3")):
            P.op("gpsimd", lambda g, sh=sh, ag=ag: g.collective_compute(
                "AllGather", ALU.bypass, replica_groups=[list(range(NC_))],
                ins=[sh.ap().opt()], outs=[ag.ap().opt()]),
                waits=[QSTF], inc=cs, inc_amt=None)
        CCW = [("ccw1", 1), ("ccw2", 1), ("ccw3", 1)]

        scp_rot = Rot(2)
        p_rot = Rot(4)
        PSL = 1024

        ctx_free = [("pe", 0), ("pe", 0)]
        den_free = [thr_consumed[0], ("pe", 0)]
        norm_stage_free = [("cst0", 0), ("cst1", 0)]
        den_sb_busy = [("pe", 0), ("pe", 0)]
        scratch_busy = [("pe", 0), ("pe", 0)]
        rden_busy = [("pe", 0), ("pe", 0)]
        pending_norm = []

        def flush_norm(slack):
            while len(pending_norm) > slack:
                pending_norm.pop(0)()

        def emit_normalize(qc, h, last_pv):
            P.op("vector", lambda v: v.tensor_copy(den_sb[h][:, :], denp[h][0:1, :]),
                 waits=[("pv", last_pv), ("dve", P.cur("dve")), den_sb_busy[h]],
                 inc="dve")
            den_free[h] = ("dve", P.cur("dve"))
            P.op("vector", lambda v: v.tensor_copy(
                scratch[:, h * 512:(h + 1) * 512], ctxp[h][:, :]),
                waits=[("dve", P.cur("dve")), scratch_busy[h]], inc="dve")
            ctx_free[h] = ("dve", P.cur("dve"))

            def doit():
                P.op("vector", lambda v: v.reciprocal(rden[h][:, :], den_sb[h][:, :]),
                     waits=[("dve", P.cur("dve")), rden_busy[h]], inc="dve")
                rd_done = P.cur("dve")
                den_sb_busy[h] = ("dve", rd_done)
                si = scp_rot.idx()
                aw = scp_rot.acquire()
                P.op("tensor", lambda t, si=si: t.matmul(
                    scp_t[si][:, 0:512], onesf[:, :], rden[h][:, :],
                    start=True, stop=True),
                    waits=[("dve", rd_done)] + aw, inc="pe")
                bc_done = P.cur("pe")
                rden_busy[h] = ("pe", bc_done)
                P.op("vector", lambda v, si=si: v.tensor_tensor(
                    out=misc[:, 4096 + h * 512: 4096 + h * 512 + 512],
                    in0=scratch[:, h * 512:(h + 1) * 512],
                    in1=scp_t[si][:, 0:512], op=ALU.mult),
                    waits=[("pe", bc_done), ("dve", P.cur("dve")),
                           norm_stage_free[h]], inc="dve")
                scp_rot.commit("dve", P.cur("dve"))
                mult_done = P.cur("dve")
                scratch_busy[h] = ("dve", mult_done)
                seg = min(512, TMY)
                cs = "cst0" if h == 0 else "cst1"
                for si2 in range(512 // seg):
                    tok0 = qc * 512 + si2 * seg
                    chunk, within = tok0 // TMY, tok0 % TMY
                    P.dma("sync", cs, lambda s, ch=chunk, wi=within, si2=si2, seg=seg: s.dma_start(
                        out=ctx_sh.ap()[ch:ch + 1, :].rearrange(
                            "r (dh t) -> (r dh) t", dh=HPC * HD)[h * 128:(h + 1) * 128, wi:wi + seg],
                        in_=misc[:, 4096 + h * 512 + si2 * seg: 4096 + h * 512 + (si2 + 1) * seg]),
                        waits=[("dve", mult_done)])
                norm_stage_free[h] = (cs, P.cur(cs))
            pending_norm.append(doit)

        pair_idx = 0
        pending_dp = [None]

        def emit_dp(h, kc, psl, p_ready):
            st_flag = (kc == 0)
            sp_flag = False
            for i in range(2):
                st = st_flag and psl.pr == 0 and i == 0
                sp = psl.sp and psl.pr == 1 and i == 1
                ktg = kc * 4 + psl.pr * 2 + i
                P.op("tensor", lambda t, h=h, pv_=psl.view, i=i, st=st, sp=sp: t.matmul(
                    denp[h][0:1, :], oneskc[:, :],
                    pv_[:, i * 512:(i + 1) * 512], start=st, stop=sp),
                    waits=[p_ready] + ([den_free[h]] if st else []),
                    inc=None)
                P.op("tensor", lambda t, h=h, pv_=psl.view, i=i, ktg=ktg, st=st, sp=sp: t.matmul(
                    ctxp[h][:, :],
                    arena[:, V0 + ktg * 256 + h * 128:
                          V0 + ktg * 256 + (h + 1) * 128],
                    pv_[:, i * 512:(i + 1) * 512], start=st, stop=sp),
                    waits=([ctx_free[h]] if st else []),
                    inc="pv" if i == 1 else None)
            p_rot.commit("pv", P.cur("pv"))

        class _Psl:
            pass

        for qc in range(NQC):
            for kc in range(qc + 1):
                for h in range(2):
                    for pr in range(2):
                        g = pair_idx
                        si = scp_rot.idx()
                        scp = scp_t[si]
                        aw = scp_rot.acquire()
                        for i in range(2):
                            kt = pr * 2 + i
                            P.op("tensor", lambda t, h=h, qc=qc, kc=kc, kt=kt, i=i, scp=scp: t.matmul(
                                scp[:, i * 512:(i + 1) * 512],
                                arena[:, K0 + h * S + kc * 512 + kt * 128:
                                      K0 + h * S + kc * 512 + (kt + 1) * 128],
                                arena[:, Q0 + h * S + qc * 512: Q0 + h * S + (qc + 1) * 512],
                                start=True, stop=True),
                                waits=(aw + [("pe", qkv_pe), ("act", qkv_act),
                                             ("dve", qkv_dve), CONS])
                                if i == 0 else [],
                                inc="sc" if i == 1 else None)
                        sc_done = P.cur("sc")
                        if pending_dp[0] is not None:
                            pending_dp[0]()
                            pending_dp[0] = None
                        pw = p_rot.acquire()
                        psl = misc[:, (g % 4) * PSL:(g % 4 + 1) * PSL]
                        P.op("scalar", lambda sc_, scp=scp, psl=psl: sc_.activation(
                            out=psl, in_=scp[:, :], func=AF.Exp, scale=SM_SCALE),
                            waits=[("sc", sc_done)] + pw, inc="exp")
                        scp_rot.commit("exp", P.cur("exp"))
                        p_ready = ("exp", P.cur("exp"))
                        if kc == qc:
                            P.op("vector", lambda v, psl=psl, pr=pr: v.tensor_tensor(
                                out=psl, in0=psl,
                                in1=maskd[:, pr * 1024:(pr + 1) * 1024], op=ALU.mult),
                                waits=[("exp", P.cur("exp")), ("dve", P.cur("dve")),
                                       CONS],
                                inc="msk")
                            p_ready = ("msk", P.cur("msk"))
                        pobj = _Psl()
                        pobj.view = psl
                        pobj.pr = pr
                        pobj.sp = (kc == qc)
                        pending_dp[0] = (lambda h=h, kc=kc, pobj=pobj, pr_=p_ready:
                                         emit_dp(h, kc, pobj, pr_))
                        pair_idx += 1
                    if kc == qc:
                        if pending_dp[0] is not None:
                            pending_dp[0]()
                            pending_dp[0] = None
                        emit_normalize(qc, h, P.cur("pv"))
                if kc < qc:
                    flush_norm(0)
        flush_norm(0)
        CST = [("cst0", P.cur("cst0")), ("cst1", P.cur("cst1"))]
        attn_dve = P.cur("dve")

        def a2a_fn(g):
            with allow_ici_all_to_all():
                return g.collective_compute(
                    "AllToAll", ALU.bypass, replica_groups=[list(range(NC_))],
                    ins=[ctx_sh.ap().opt()], outs=[ctx_a2a.ap().opt()])
        P.op("gpsimd", a2a_fn, waits=CST + CCW, inc="cca2a", inc_amt=None)
        CCA2A = ("cca2a", 1)

        attn_done = [("pv", P.cur("pv")), ("exp", P.cur("exp")),
                     ("msk", P.cur("msk"))]
        arena_tap = []
        if DEBUG_TAPS:
            arena_tap = [P.dma("gpsimd", "dbg", lambda g: g.dma_start(
                out=dbg_qkvsb.ap(), in_=arena[:, :]),
                waits=attn_done + [("dve", P.cur("dve"))])]
        for r in range(NC_):
            for dc in range(2):
                P.dma("sync", "ctxld", lambda s, r=r, dc=dc: s.dma_start(
                    out=arena[:, (r * 2 + dc) * TMY:(r * 2 + dc + 1) * TMY],
                    in_=ctx_a2a.ap()[r:r + 1, :].rearrange(
                        "r (dh t) -> (r dh) t", dh=HPC * HD)[dc * 128:(dc + 1) * 128, :]),
                    waits=[CCA2A] + attn_done + arena_tap)
        CTXLD = ("ctxld", P.cur("ctxld"))

        wo_rot = Rot(2)
        op_free = {}
        res_rot = Rot(2)
        for do in range(NDO):
            sslot = wo_rot.idx()
            sview = stream[:, sslot * 8192: sslot * 8192 + NDC * 512]
            wosem = "wo0" if sslot == 0 else "wo1"
            wo_ld = P.dma("gpsimd", wosem, lambda g, do=do, sv=sview: g.dma_start(
                out=sv.rearrange("p (a b) -> p a b", b=512),
                in_=wout[:, do * 512:(do + 1) * 512].rearrange(
                    "(dc p) o -> p dc o", p=128)),
                waits=wo_rot.acquire() + [("pe", qkv_pe)])
            for tt in range(NTT):
                bank = tt % 8
                ps = pb[bank]
                aw = [op_free[bank]] if bank in op_free else []
                for dh in range(NDC):
                    P.op("tensor", lambda t, dh=dh, tt=tt, ps=ps, sv=sview: t.matmul(
                        ps, arena[:, dh * TMY + tt * 128: dh * TMY + (tt + 1) * 128],
                        sv[:, dh * 512:(dh + 1) * 512],
                        start=(dh == 0), stop=False),
                        waits=([CTXLD, wo_ld] + aw) if dh == 0 else [],
                        inc=None)
                P.op("tensor", lambda t, do=do, ps=ps: t.matmul(
                    ps, onesr[:, :], bout_bf[0:1, do * 512:(do + 1) * 512],
                    start=False, stop=True),
                    waits=[GCONS], inc="pe")
                mm_done = P.cur("pe")
                rslot = res_rot.idx()
                rview = scratch[:, rslot * 512:(rslot + 1) * 512]
                rsem = "rp0" if rslot == 0 else "rp1"
                ld = P.dma("sync", rsem, lambda s, tt=tt, do=do, rv=rview: s.dma_start(
                    out=rv, in_=x_my[tt * 128:(tt + 1) * 128, do * 512:(do + 1) * 512]),
                    waits=res_rot.acquire() + [("dve", attn_dve)])
                P.op("vector", lambda v, ps=ps, rv=rview: v.tensor_tensor(
                    out=rv, in0=ps, in1=rv, op=ALU.add),
                    waits=[("pe", mm_done), ld, ("dve", P.cur("dve"))],
                    inc="dve")
                op_free[bank] = ("dve", P.cur("dve"))
                st = P.dma("sync", "x1st0" if rslot == 0 else "x1st1",
                           lambda s, tt=tt, do=do, rv=rview: s.dma_start(
                    out=x1res_d.ap()[tt * 128:(tt + 1) * 128, do * 512:(do + 1) * 512],
                    in_=rv),
                    waits=[("dve", P.cur("dve"))])
                res_rot.commit(*st)
            wo_rot.commit("pe", P.cur("pe"))
        X1ST = [("x1st0", P.cur("x1st0")), ("x1st1", P.cur("x1st1"))]

        xtr_rot = bf_rot
        xn2_stage_free = ("pe", 0)
        pending_xtr = [None]

        def flush_xtr():
            fn = pending_xtr[0]
            pending_xtr[0] = None
            if fn is not None:
                fn()

        for tt in range(NTT):
            slot = (tt % 2) * 2048
            tile_done = []
            for s_ in range(2):
                idxs = [do * NTT + tt for do in range(NDO)
                        if (do * NTT + tt) % 2 == s_]
                if idxs:
                    tile_done.append((f"x1st{s_}", 16 * (max(idxs) // 2 + 1)))
            ld = P.dma("sync", "xld", lambda s, tt=tt: s.dma_start(
                out=scratch[:, 2048:4096],
                in_=x1res_d.ap()[tt * 128:(tt + 1) * 128, :]),
                waits=tile_done + [("act", P.cur("act")), ("dve", P.cur("dve"))])
            a_done, _ = emit_ln_tile(misc[:, slot:slot + 2048], [ld],
                                     apply_waits=[xn2_stage_free], slot=1)

            def emit_xtr(tt=tt, slot=slot, a_done=a_done):
                for dc in range(NDC):
                    tview = pbbf[xtr_rot.idx()][:, 0:128]
                    aw = xtr_rot.acquire()
                    if tt == 0 and dc < 2:
                        bank = 0 if dc == 0 else 2
                        if bank in op_free:
                            aw = aw + [op_free[bank]]
                    P.op("tensor", lambda t, dc=dc, tv=tview, slot=slot: t.transpose(
                        tv, misc[:, slot + dc * 128: slot + (dc + 1) * 128],
                        ident[:, :]),
                        waits=[("act", a_done)] + aw, inc="pe")
                    tr = P.cur("pe")
                    P.op("vector", lambda v, dc=dc, tt=tt, tv=tview: v.tensor_copy(
                        arena[:, XT0 + dc * TMY + tt * 128: XT0 + dc * TMY + (tt + 1) * 128],
                        tv),
                        waits=[("pe", tr), ("dve", P.cur("dve"))], inc="dve")
                    xtr_rot.commit("dve", P.cur("dve"))
            prev = pending_xtr[0]
            pending_xtr[0] = emit_xtr
            if prev is not None:
                prev()
                xn2_stage_free = ("pe", P.cur("pe"))
        flush_xtr()
        xn2_stage_free = ("pe", P.cur("pe"))
        xn2_dve = P.cur("dve")
        xn2_pe = P.cur("pe")
        xn2_act = P.cur("act")

        w12_rot = Rot(2)
        ff_free = {}
        hstage_free = [("pe", 0), ("pe", 0)]
        sigstage_free = [("pe", 0), ("pe", 0)]
        for j in range(NJC):
            sslot = w12_rot.idx()
            w1v = stream[:, sslot * 4096: sslot * 4096 + 2048]
            w2v = stream[:, sslot * 4096 + 2048: sslot * 4096 + 4096]
            wlsem = "wl0" if sslot == 0 else "wl1"
            aw = w12_rot.acquire()
            P.dma("sync", wlsem, lambda s, j=j, wv=w1v: s.dma_start(
                out=wv.rearrange("p (a b) -> p a b", b=128),
                in_=w1t_ag.ap()[:, j * 128:(j + 1) * 128].rearrange(
                    "(dc p) j2 -> p dc j2", p=128)),
                waits=CCW[:1] + aw)
            ld = P.dma("sync", wlsem, lambda s, j=j, wv=w2v: s.dma_start(
                out=wv.rearrange("p (a b) -> p a b", b=128),
                in_=w2t_ag.ap()[:, j * 128:(j + 1) * 128].rearrange(
                    "(dc p) j2 -> p dc j2", p=128)),
                waits=CCW[1:2])
            last_pe = 0
            for tc in range(NTC):
                for wi, wv in ((0, w1v), (1, w2v)):
                    bank = (j % 2) * 4 + tc * 2 + wi
                    ps = pb[bank]
                    awf = [ff_free[bank]] if bank in ff_free else \
                        [op_free.get(bank, ("pe", 0))]
                    for dc in range(NDC):
                        P.op("tensor", lambda t, dc=dc, tc=tc, ps=ps, wv=wv: t.matmul(
                            ps[:, 0:TCH], wv[:, dc * 128:(dc + 1) * 128],
                            arena[:, XT0 + dc * TMY + tc * TCH:
                                  XT0 + dc * TMY + tc * TCH + TCH],
                            start=(dc == 0), stop=(dc == NDC - 1)),
                            waits=([ld, ("dve", xn2_dve), ("pe", xn2_pe)]
                                   + awf) if dc == 0 else [],
                            inc="pe" if dc == NDC - 1 else None)
                    last_pe = P.cur("pe")
                    par = j % 2
                    sig_off = par * 2048
                    hid_off = 4096 + par * TMY
                    if wi == 0:
                        P.op("scalar", lambda sc, ps=ps, tc=tc, so=sig_off: sc.activation(
                            out=misc[:, so + tc * TCH: so + tc * TCH + TCH],
                            in_=ps[:, 0:TCH], func=AF.Sigmoid),
                            waits=[("pe", last_pe), ("act", P.cur("act")),
                                   sigstage_free[par]], inc="act")
                        ff_free[bank] = ("act", P.cur("act"))
                    else:
                        P.op("vector", lambda v, ps=ps, tc=tc, so=sig_off, ho=hid_off: v.tensor_tensor(
                            out=misc[:, ho + tc * TCH: ho + tc * TCH + TCH],
                            in0=misc[:, so + tc * TCH: so + tc * TCH + TCH],
                            in1=ps[:, 0:TCH], op=ALU.mult),
                            waits=[("pe", last_pe), ("act", P.cur("act")),
                                   ("dve", P.cur("dve")), hstage_free[par]] + CST,
                            inc="dve")
                        ff_free[bank] = ("dve", P.cur("dve"))
                        sigstage_free[par] = ("dve", P.cur("dve"))
            hid_done = P.cur("dve")
            hsem = "hst0" if j % 2 == 0 else "hst1"
            hstage_free[j % 2] = P.dma("gpsimd", hsem, lambda g, j=j: g.dma_start(
                out=hidden_d.ap()[j * 128:(j + 1) * 128, :],
                in_=misc[:, 4096 + (j % 2) * TMY:4096 + (j % 2) * TMY + TMY]),
                waits=[("dve", hid_done)])
            w12_rot.commit("pe", last_pe)
        HST = [("hst0", P.cur("hst0")), ("hst1", P.cur("hst1"))]

        hb_rot = Rot(4)
        w3_rot = Rot(4)
        x3_free = {}
        first5b = [("act", xn2_act), ("dve", P.cur("dve"))]
        rslot_free = {tt: first5b for tt in range(NTT)}
        rsems = ["rp0", "rp1", "rp2", "rp3", "rp4", "rp5", "rp6", "rp7"]
        for do in range(NDO):
            piece_lds = {}
            for tt in range(NTT):
                rview = scratch[:, (tt % 8) * 512:((tt % 8) + 1) * 512]
                piece_lds[tt] = P.dma("sync", rsems[tt % 8],
                                      lambda s, tt=tt, do=do, rv=rview: s.dma_start(
                    out=rv, in_=x1res_d.ap()[tt * 128:(tt + 1) * 128,
                                             do * 512:(do + 1) * 512]),
                    waits=list(rslot_free[tt]) + X1ST)
            for j in range(NJC):
                hslot = hb_rot.idx()
                hview = misc[:, hslot * TMY:(hslot + 1) * TMY]
                hsem = ["hb0", "hb1", "hb2", "hb3"][hslot]
                hw = [("hst0" if j % 2 == 0 else "hst1", 16 * (j // 2 + 1))]
                ld_h = P.dma("sync", hsem, lambda s, j=j, hv=hview: s.dma_start(
                    out=hv, in_=hidden_d.ap()[j * 128:(j + 1) * 128, :]),
                    waits=hw + hb_rot.acquire())
                w3slot = w3_rot.idx()
                w3s = stream[:, 8192 + w3slot * 512: 8192 + (w3slot + 1) * 512]
                wsem = ["w30", "w31", "w32", "w33"][w3slot]
                ld_w = P.dma("scalar", wsem, lambda s, j=j, do=do, wv=w3s: s.dma_start(
                    out=wv, in_=w3t_ag.ap()[j * 128:(j + 1) * 128,
                                            do * 512:(do + 1) * 512]),
                    waits=w3_rot.acquire() + CCW[2:3])
                for tt in range(NTT):
                    bank = tt % 8
                    ps = pb[bank]
                    awf = ([x3_free[bank]] if bank in x3_free else
                           [ff_free.get(bank, ("pe", 0))]) if j == 0 else []
                    P.op("tensor", lambda t, tt=tt, ps=ps, hv=hview, wv=w3s,
                         st=(j == 0), sp=(j == NJC - 1): t.matmul(
                        ps, hv[:, tt * 128:(tt + 1) * 128], wv,
                        start=st, stop=sp),
                        waits=[ld_h, ld_w] + awf,
                        inc="pe" if tt == NTT - 1 else None)
                last = P.cur("pe")
                hb_rot.commit("pe", last)
                w3_rot.commit("pe", last)
            x3_pe = P.cur("pe")
            for tt in range(NTT):
                bank = tt % 8
                ps = pb[bank]
                rview = scratch[:, (tt % 8) * 512:((tt % 8) + 1) * 512]
                P.op("vector", lambda v, ps=ps, rv=rview: v.tensor_tensor(
                    out=rv, in0=ps, in1=rv, op=ALU.add),
                    waits=[("pe", x3_pe), piece_lds[tt], ("dve", P.cur("dve"))],
                    inc="dve")
                x3_free[bank] = ("dve", P.cur("dve"))
                st = P.dma("sync", rsems[tt % 8],
                           lambda s, tt=tt, do=do, rv=rview: s.dma_start(
                    out=y_my[tt * 128:(tt + 1) * 128, do * 512:(do + 1) * 512],
                    in_=rv),
                    waits=[("dve", P.cur("dve"))])
                rslot_free[tt] = [st]

        if DEBUG_TAPS:
            for dst, srcT, w in (
                    (dbg_ctx, ctx_a2a, [CCA2A]),
                    (dbg_ctxsh, ctx_sh, [CCA2A]),
                    (dbg_x1res, x1res_d, X1ST),
                    (dbg_hid, hidden_d, HST),
                    (dbg_w1t, w1t_ag, CCW[:1]),
                    (dbg_w3t, w3t_ag, CCW[2:3]),
                    (dbg_gam, gam_out, [("ccgam", 1)])):
                P.dma("gpsimd", "dbg", lambda g, d=dst, s2=srcT: g.dma_start(
                    out=d.ap(), in_=s2.ap()), waits=list(w))

        finals = [(s, P.cur(s)) for s in DSEMS + CSEMS if P.cur(s) > 0]
        P.op("sync", lambda s: s.nop(), waits=finals)

        with nc.Block() as block:
            @block.tensor
            def _(t):
                P.run("tensor", t, sems)

            @block.scalar
            def _(sc):
                P.run("scalar", sc, sems)

            @block.vector
            def _(v):
                P.run("vector", v, sems)

            @block.gpsimd
            def _(g):
                P.run("gpsimd", g, sems)

            @block.sync
            def _(s):
                P.run("sync", s, sems)

    return nc


def make_inputs(x, Wqkv, Wout, b_out, w1, w2, w3, S, HFF):
    TMY = S // NC_
    W12SH = D // NC_
    W3SH = HFF // NC_
    bf = ml_dtypes.bfloat16
    maskd = np.triu(np.ones((512, 512), np.float32)).astype(bf)
    ident = np.eye(128, dtype=bf)
    oneskc = np.ones((128, 1), bf)
    onesr = np.ones((1, 128), bf)
    onesf = np.ones((1, 128), np.float32)
    in_maps = []
    for c in range(NC_):
        h0, h1 = 2 * c, 2 * c + 1
        wq = np.concatenate([
            Wqkv[:, h0 * HD:(h0 + 1) * HD], Wqkv[:, h1 * HD:(h1 + 1) * HD],
            Wqkv[:, D + h0 * HD: D + (h0 + 1) * HD],
            Wqkv[:, D + h1 * HD: D + (h1 + 1) * HD],
            Wqkv[:, 2 * D + h0 * HD: 2 * D + (h0 + 1) * HD],
            Wqkv[:, 2 * D + h1 * HD: 2 * D + (h1 + 1) * HD],
        ], axis=1)
        in_maps.append({
            "x_my": np.ascontiguousarray(x[c * TMY:(c + 1) * TMY, :]),
            "wqkv_my": np.ascontiguousarray(wq),
            "wout": np.ascontiguousarray(Wout),
            "bout": np.ascontiguousarray(b_out.reshape(1, D)),
            "w1_my": np.ascontiguousarray(w1[:, c * W12SH:(c + 1) * W12SH]),
            "w2_my": np.ascontiguousarray(w2[:, c * W12SH:(c + 1) * W12SH]),
            "w3_my": np.ascontiguousarray(w3[:, c * W3SH:(c + 1) * W3SH]),
            "ident": ident, "maskd": maskd, "oneskc": oneskc,
            "onesr": onesr, "onesf": onesf,
        })
    return in_maps


def run(x, Wqkv, Wout, b_out, w1, w2, w3, S, HFF, full_results=False):
    in_maps = make_inputs(x, Wqkv, Wout, b_out, w1, w2, w3, S, HFF)
    nc = build_program(S, HFF)
    res = run_bass_kernel_spmd(nc, in_maps, core_ids=list(range(NC_)))
    y = np.concatenate([res.results[c]["y_my"] for c in range(NC_)], axis=0)
    if full_results:
        return y, res.results
    return y


def kernel(L0_attn_x, L0_attn_W_qkv, L0_attn_W_out, L0_attn_b_out,
           L0_ffn_w1, L0_ffn_w2, L0_ffn_w3):
    S, HFF = 8192, 5632
    y = run(np.asarray(L0_attn_x, np.float32).reshape(S, D),
            np.asarray(L0_attn_W_qkv, np.float32),
            np.asarray(L0_attn_W_out, np.float32),
            np.asarray(L0_attn_b_out, np.float32),
            np.asarray(L0_ffn_w1, np.float32),
            np.asarray(L0_ffn_w2, np.float32),
            np.asarray(L0_ffn_w3, np.float32),
            S, HFF)
    return y.reshape(1, S, D).astype(np.float32)
